# version 02
# baseline (speedup 1.0000x reference)
import numpy as np
import ml_dtypes
from contextlib import ExitStack

import concourse.bass as bass
import concourse.mybir as mybir
from concourse.bass_core_rust import IndirectOffsetOnAxis
from concourse.bass_utils import run_bass_kernel_spmd

F32 = mybir.dt.float32
BF16 = mybir.dt.bfloat16
I32 = mybir.dt.int32
AF = mybir.ActivationFunctionType
ALU = mybir.AluOpType
AX = mybir.AxisListType

T = 2048
E = 1024
NT = T // 128
NE = E // 128
NEG = -60000.0
bfdt = ml_dtypes.bfloat16

ENGINES = ("sync", "tensor", "vector", "scalar", "gpsimd")


def tile_lhsT(WT):
    C, P = WT.shape
    cc = C // 128
    return np.ascontiguousarray(WT.reshape(cc, 128, P).transpose(1, 0, 2)
                                .reshape(128, cc * P)).astype(np.float32)

def nat_tiles(x):
    R, F = x.shape
    nt = R // 128
    return np.ascontiguousarray(x.reshape(nt, 128, F).transpose(1, 0, 2)
                                .reshape(128, nt * F)).astype(np.float32)

def col128(v):
    v = np.asarray(v).reshape(-1)
    k = max(1, v.shape[0] // 128)
    return np.ascontiguousarray(v.reshape(k, 128).T).astype(np.float32)



class Step:
    __slots__ = ("eng", "fn", "after", "k", "idx", "target", "semkey")
    def __init__(self, eng, fn, after, k):
        self.eng = eng
        self.fn = fn
        self.after = after
        self.k = k
        self.idx = None
        self.semkey = None
        self.target = None


class Prog:
    NDMA = 8

    def __init__(self, name="prog"):
        self.nc = bass.Bass("TRN2", target_bir_lowering=False)
        self.es = ExitStack()
        self.steps = []
        self.sb = {}
        self._dma_rr = {}
        self._dma_prev = {}

    def sbuf(self, name, shape, dtype):
        h = self.es.enter_context(self.nc.sbuf_tensor(name, shape, dtype))
        self.sb[name] = h
        return h

    def psum(self, name, shape, dtype):
        h = self.es.enter_context(self.nc.psum_tensor(name, shape, dtype))
        self.sb[name] = h
        return h

    def step(self, eng, fn, after=(), dma=False):
        s = Step(eng, fn, [a for a in after if a is not None], 16 if dma else 1)
        if dma:
            r = self._dma_rr.get(eng, 0)
            self._dma_rr[eng] = (r + 1) % self.NDMA
            s.semkey = ("D", eng, r)
            prevd = self._dma_prev.get(s.semkey)
            if prevd is not None:
                s.after.append(prevd)
            self._dma_prev[s.semkey] = s
        else:
            s.semkey = ("E", eng, 0)
        self.steps.append(s)
        return s

    def dma(self, eng, out, in_, after=(), **kw):
        return self.step(eng, lambda e: e.dma_start(out=out, in_=in_, **kw),
                         after=after, dma=True)

    def build(self):
        nc = self.nc
        semkeys = sorted({s.semkey for s in self.steps})
        sems = {}
        with ExitStack() as bes:
            for kk in semkeys:
                sems[kk] = bes.enter_context(nc.semaphore("_".join(map(str, kk))))
            counts = {kk: 0 for kk in semkeys}
            for s in self.steps:
                counts[s.semkey] += s.k
                s.target = counts[s.semkey]
            blk = bes.enter_context(nc.Block())
            per_eng = {en: [s for s in self.steps if s.eng == en] for en in ENGINES}

            def mk(en):
                def sec(eng):
                    waited = {kk: 0 for kk in semkeys}
                    for s in per_eng[en]:
                        for d in s.after:
                            if d.target > waited[d.semkey]:
                                eng.wait_ge(sems[d.semkey], d.target)
                                waited[d.semkey] = d.target
                        ins = s.fn(eng)
                        ins.then_inc(sems[s.semkey], s.k)
                    return
                return sec
            for en in ENGINES:
                if per_eng[en]:
                    getattr(blk, en)(mk(en))
        self.es.close()
        return nc



def alloc_common(p, need_masks=()):
    p.psum("psA", [128, 2048], F32)
    p.psum("psB", [128, 2048], BF16)
    p.psum("psC", [128, 1024], F32)
    p.sbuf("idn", [128, 128], BF16)
    p.sbuf("ones_row", [1, 128], BF16)
    p.sbuf("stat", [128, 24], F32)
    p.sbuf("scr", [128, 1024], F32)
    p.sbuf("Pn", [128, 2, 2048], BF16)
    p.sbuf("Pn2", [128, 1, 2048], BF16)
    p.sbuf("rmax", [128, 16], F32)
    for m in need_masks:
        p.sbuf(m, [128, 128], BF16)
    c_idn = p.nc.dram_tensor("c_idn", [128, 128], F32, kind="ExternalInput").ap()
    ld = {}
    ld["idn"] = p.dma("gpsimd", p.sb["idn"][:, :], c_idn)
    for m in need_masks:
        cm = p.nc.dram_tensor("c_" + m, [128, 128], F32, kind="ExternalInput").ap()
        ld[m] = p.dma("gpsimd", p.sb[m][:, :], cm)
    ld["ones"] = p.step("vector", lambda v: v.memset(p.sb["ones_row"][:, :], 1.0))
    return ld


def emit_ln(p, xf_fn, xln, nt=NT, width=E, after=(), x_tile_after=None):
    stat, scr = p.sb["stat"], p.sb["scr"]
    applies = []
    for i in range(nt):
        o = (i % 2) * 12
        dep = list(after)
        if x_tile_after is not None:
            dep.append(x_tile_after(i))
        if i >= 2:
            dep.append(applies[i - 2])

        def stats(v, i=i, o=o):
            v.scalar_tensor_tensor(scr[:, 0:width], xf_fn(i), 1.0, xf_fn(i),
                                   ALU.mult, ALU.mult, accum_out=stat[:, o + 1:o + 2])
            v.tensor_reduce(stat[:, o:o + 1], xf_fn(i), AX.X, ALU.add)
            v.tensor_scalar(stat[:, o + 2:o + 3], stat[:, o:o + 1], 1.0 / width, None, ALU.mult)
            v.tensor_scalar(stat[:, o + 3:o + 4], stat[:, o + 1:o + 2], 1.0 / width, None, ALU.mult)
            v.scalar_tensor_tensor(stat[:, o + 4:o + 5], stat[:, o + 2:o + 3], 1.0,
                                   stat[:, o + 2:o + 3], ALU.mult, ALU.mult)
            v.tensor_tensor(stat[:, o + 5:o + 6], stat[:, o + 3:o + 4],
                            stat[:, o + 4:o + 5], ALU.subtract)
            return v.tensor_scalar(stat[:, o + 6:o + 7], stat[:, o + 5:o + 6],
                                   1e-5, None, ALU.add)
        st = p.step("vector", stats, after=dep)

        def apply(s, i=i, o=o):
            s.activation(stat[:, o + 7:o + 8], stat[:, o + 6:o + 7], AF.Ln)
            s.activation(stat[:, o + 8:o + 9], stat[:, o + 7:o + 8], AF.Exp, scale=-0.5)
            s.mul(stat[:, o + 9:o + 10], stat[:, o + 2:o + 3], stat[:, o + 8:o + 9])
            s.mul(stat[:, o + 10:o + 11], stat[:, o + 9:o + 10], -1.0)
            return s.activation(xln(i), xf_fn(i), AF.Identity,
                                bias=stat[:, o + 10:o + 11], scale=stat[:, o + 8:o + 9])
        applies.append(p.step("scalar", apply, after=[st]))
    return applies


def emit_transpose(p, src_fn, dst_fn, nt, nchunk, after_tile=None, after=()):
    psB, idn = p.sb["psB"], p.sb["idn"]
    evacs = []
    for i in range(nt):
        half = (i % 2) * nchunk * 128
        dep = list(after)
        if after_tile is not None:
            dep.append(after_tile(i))
        if i >= 2:
            dep.append(evacs[i - 2])

        def tr(t, i=i, half=half):
            for ch in range(nchunk):
                ins = t.transpose(psB[:, half + ch * 128: half + (ch + 1) * 128],
                                  src_fn(i, ch), idn[:, :])
            return ins
        ts = p.step("tensor", tr, after=dep)

        def ev(v, i=i, half=half):
            return v.tensor_copy(dst_fn(i),
                                 psB[:, half:half + nchunk * 128]
                                 .rearrange("p (c q) -> p c q", q=128))
        evacs.append(p.step("vector", ev, after=[ts]))
    return evacs


def emit_projT(p, w_img, rhsT, out, nob, bias_col=None, after=()):
    psA = p.sb["psA"]
    mms, evs = [], []
    for ob in range(nob):
        dep = list(after)
        if ob >= 1:
            dep.append(evs[ob - 1])

        def mm(t, ob=ob):
            for th in range(4):
                for ec in range(NE):
                    ins = t.matmul(psA[:, th * 512:(th + 1) * 512],
                                   w_img[:, ec, ob * 128:(ob + 1) * 128],
                                   rhsT[:, ec, th * 512:(th + 1) * 512],
                                   start=(ec == 0), stop=(ec == NE - 1))
            return ins
        m = p.step("tensor", mm, after=dep)
        mms.append(m)

        def ev(s, ob=ob):
            if bias_col is not None:
                return s.activation(out[:, ob, :], psA[:, :], AF.Identity,
                                    bias=bias_col[:, ob:ob + 1])
            return s.activation(out[:, ob, :], psA[:, :], AF.Copy)
        evs.append(p.step("scalar", ev, after=[m]))
    return mms, evs


def emit_attn_head(p, qt_lhs, kt_rhs, vn_lhs, scale, nqt, kmax_fn, diag_extra=None,
                   pv_psum="psC", pv_parts=64, pv_width=1024, pv_q0=0,
                   pv_start=True, pv_stop=True, after=(), mask_fn=None,
                   prev_pv=None, causal=False):
    psA, psB = p.sb["psA"], p.sb["psB"]
    Pn, Pn2, rmax, idn, PT = (p.sb[k] for k in ("Pn", "Pn2", "rmax", "idn", "PT"))
    pv = p.sb[pv_psum]
    sc_s, rm_s, ex_s, dg_s, tr_s, pt_s = {}, {}, {}, {}, {}, {}
    for qt in range(nqt):
        khi = kmax_fn(qt)
        nkc = khi // 128
        dep = list(after)
        if qt >= 1:
            dep.append(ex_s[qt - 1])

        def sc(t, qt=qt, khi=khi):
            nmm = (khi + 511) // 512
            for m in range(nmm):
                k0, k1 = m * 512, min((m + 1) * 512, khi)
                ins = t.matmul(psA[:, k0:k1], qt_lhs(qt), kt_rhs(k0, k1),
                               start=True, stop=True)
            if mask_fn is not None:
                mk, kpos = mask_fn(qt)
                ins = t.matmul(psA[:, kpos:kpos + 128], idn[:, :], mk,
                               start=False, stop=True, skip_group_check=True)
            return ins
        sc_s[qt] = p.step("tensor", sc, after=dep)

        o = (qt % 2) * 8
        dep = [sc_s[qt]]
        if qt >= 2:
            dep.append(dg_s[qt - 2])

        rm0 = p.step("vector",
                     lambda v, o=o, khi=khi: v.tensor_reduce(rmax[:, o:o + 1],
                                                             psA[:, 0:khi], AX.X, ALU.max),
                     after=dep)
        rm_s[qt] = p.step("vector",
                          lambda v, o=o: v.tensor_scalar(rmax[:, o + 1:o + 2],
                                                         rmax[:, o:o + 1],
                                                         -scale, None, ALU.mult),
                          after=[rm0])

        dep = [rm_s[qt]]
        if qt >= 2:
            dep.append(tr_s[qt - 2])

        def ex(s, qt=qt, o=o, khi=khi):
            return s.activation(Pn[:, qt % 2, 0:khi], psA[:, 0:khi], AF.Exp,
                                scale=scale, bias=rmax[:, o + 1:o + 2],
                                accum_out=rmax[:, o + 3:o + 4])
        ex_s[qt] = p.step("scalar", ex, after=dep)

        dep = [ex_s[qt]]
        if qt >= 1:
            dep.append(tr_s[qt - 1])

        def dg0(v, qt=qt, o=o):
            ins = v.reciprocal(rmax[:, o + 2:o + 3], rmax[:, o + 3:o + 4])
            return ins
        d0 = p.step("vector", dg0, after=dep)
        if diag_extra is not None:
            d0 = p.step("vector",
                        lambda v, o=o: v.tensor_tensor(rmax[:, o + 2:o + 3],
                                                       rmax[:, o + 2:o + 3],
                                                       diag_extra, ALU.mult),
                        after=[d0])
        dg_s[qt] = p.step("vector",
                          lambda v, qt=qt, o=o, khi=khi: v.tensor_scalar(
                              Pn2[:, 0, 0:khi], Pn[:, qt % 2, 0:khi],
                              rmax[:, o + 2:o + 3], None, ALU.mult),
                          after=[d0])

        dep = [dg_s[qt]]
        if qt >= 1:
            dep.append(pt_s[qt - 1])

        def tr(t, qt=qt, nkc=nkc):
            for kc in range(nkc):
                ins = t.transpose(psB[:, kc * 128:(kc + 1) * 128],
                                  Pn2[:, 0, kc * 128:(kc + 1) * 128],
                                  idn[:, :])
            return ins
        tr_s[qt] = p.step("tensor", tr, after=dep)

        def ptv(v, qt=qt, nkc=nkc):
            return v.tensor_copy(PT[:, 0:nkc, qt * 128:(qt + 1) * 128],
                                 psB[:, 0:nkc * 128].rearrange("p (c q) -> p c q", q=128))
        pt_s[qt] = p.step("vector", ptv, after=[tr_s[qt]])

    nkc_last = kmax_fn(nqt - 1) // 128
    dep = [pt_s[q] for q in range(nqt)] + list(after)
    if prev_pv is not None:
        dep.append(prev_pv)

    def pvf(t):
        if not causal:
            for kc in range(nkc_last):
                for qc in range(pv_width // 512):
                    ins = t.matmul(pv[0:pv_parts,
                                      pv_q0 + qc * 512:pv_q0 + (qc + 1) * 512],
                                   vn_lhs(kc), PT[:, kc, qc * 512:(qc + 1) * 512],
                                   start=(kc == 0 and pv_start),
                                   stop=(kc == nkc_last - 1 and pv_stop),
                                   skip_group_check=True)
        else:
            for qt in range(nqt):
                nkc = kmax_fn(qt) // 128
                for kc in range(nkc):
                    ins = t.matmul(pv[0:pv_parts,
                                      pv_q0 + qt * 128:pv_q0 + (qt + 1) * 128],
                                   vn_lhs(kc), PT[:, kc, qt * 128:(qt + 1) * 128],
                                   start=(kc == 0 and pv_start),
                                   stop=(kc == nkc - 1 and pv_stop),
                                   skip_group_check=True)
        return ins
    return p.step("tensor", pvf, after=dep)




def build_l0_attn_b():
    H, D = 16, 64
    scale = 1.0 / np.sqrt(D)
    p = Prog("l0b")
    nc = p.nc
    x_in = nc.dram_tensor("x", [128, NT * E], F32, kind="ExternalInput").ap()
    wqd = nc.dram_tensor("wqT", [128, NE * E], F32, kind="ExternalInput").ap()
    wkd = nc.dram_tensor("wkT", [128, NE * E], F32, kind="ExternalInput").ap()
    wvd = nc.dram_tensor("wvT", [128, NE * E], F32, kind="ExternalInput").ap()
    wod = nc.dram_tensor("woT", [128, NE * E], F32, kind="ExternalInput").ap()
    bqd = nc.dram_tensor("bq", [128, NE], F32, kind="ExternalInput").ap()
    bkd = nc.dram_tensor("bk", [128, NE], F32, kind="ExternalInput").ap()
    bvd = nc.dram_tensor("bv", [128, NE], F32, kind="ExternalInput").ap()
    bod = nc.dram_tensor("bo", [1, E], F32, kind="ExternalInput").ap()
    rcd_d = nc.dram_tensor("ropecos", [128, NT * 512], F32, kind="ExternalInput").ap()
    rsd_d = nc.dram_tensor("ropesin", [128, NT * 512], F32, kind="ExternalInput").ap()
    x_out = nc.dram_tensor("xo", [128, NT * E], F32, kind="ExternalOutput").ap()

    ld = alloc_common(p)
    QT = p.sbuf("QT", [128, NE, T], BF16)
    KT = p.sbuf("KT", [128, NE, T], BF16)
    Vn = p.sbuf("Vn", [128, NT, E], BF16)
    wb1 = p.sbuf("wb1", [128, NE, E], BF16)
    bq_s = p.sbuf("bq_s", [128, NE], F32)
    bk_s = p.sbuf("bk_s", [128, NE], F32)
    bv_s = p.sbuf("bv_s", [128, NE], F32)
    bv_b = p.sbuf("bv_b", [128, NE], BF16)
    bo_s = p.sbuf("bo_s", [1, E], F32)
    bop = p.sbuf("bop", [1, E], BF16)

    l_bq = p.dma("gpsimd", bq_s[:, :], bqd)
    l_bk = p.dma("gpsimd", bk_s[:, :], bkd)
    l_bv = p.dma("gpsimd", bv_s[:, :], bvd)
    l_bo = p.dma("gpsimd", bo_s[:1, :], bod)
    l_wo = p.dma("gpsimd", wb1[:, :, :], wod)
    bvc = p.step("vector", lambda v: v.tensor_copy(bv_b[:, :], bv_s[:, :]), after=[l_bv])

    with ExitStack() as lv1:
        e1 = lv1.enter_context
        xdb = e1(nc.sbuf_tensor("xdb", [128, 2, E], F32))
        rcd = e1(nc.sbuf_tensor("rcd", [128, 2, 512], BF16))
        rsd = e1(nc.sbuf_tensor("rsd", [128, 2, 512], BF16))
        xlnb = e1(nc.sbuf_tensor("xlnb", [128, 2, E], BF16))
        rx = e1(nc.sbuf_tensor("rx", [128, 2, E], BF16))
        rxT = e1(nc.sbuf_tensor("rxT", [128, NE, T], BF16))
        wb0 = e1(nc.sbuf_tensor("wb0", [128, NE, E], BF16))

        l_w0 = p.dma("gpsimd", wb0[:, :, :], wqd)
        stat, scr = p.sb["stat"], p.sb["scr"]
        psB, idn = p.sb["psB"], p.sb["idn"]
        xls, lrc, lrs, lns_st, lns, rope_s, tr_s, trs = [], [], [], [], [], [], [], []
        for i in range(NT):
            b = i % 2
            o = b * 12
            xls.append(p.dma("sync", xdb[:, b, :], x_in[:, i * E:(i + 1) * E],
                             after=[lns[i - 2]] if i >= 2 else []))
            lrc.append(p.dma("gpsimd", rcd[:, b, :], rcd_d[:, i * 512:(i + 1) * 512],
                             after=[rope_s[i - 2]] if i >= 2 else []))
            lrs.append(p.dma("gpsimd", rsd[:, b, :], rsd_d[:, i * 512:(i + 1) * 512],
                             after=[rope_s[i - 2]] if i >= 2 else []))

            def stats1(v, i=i, o=o, b=b):
                v.scalar_tensor_tensor(scr[:, 0:E], xdb[:, b, :], 1.0, xdb[:, b, :],
                                       ALU.mult, ALU.mult, accum_out=stat[:, o + 1:o + 2])
                return v.tensor_reduce(stat[:, o:o + 1], xdb[:, b, :], AX.X, ALU.add)
            sdep = [xls[i]] + ([lns[i - 2]] if i >= 2 else [])
            s1 = p.step("vector", stats1, after=sdep)

            def stats2(v, i=i, o=o):
                v.tensor_scalar(stat[:, o + 2:o + 3], stat[:, o:o + 1], 1.0 / E, None, ALU.mult)
                return v.tensor_scalar(stat[:, o + 3:o + 4], stat[:, o + 1:o + 2],
                                       1.0 / E, None, ALU.mult)
            s2 = p.step("vector", stats2, after=[s1])

            def stats3(v, i=i, o=o):
                v.scalar_tensor_tensor(stat[:, o + 4:o + 5], stat[:, o + 2:o + 3], 1.0,
                                       stat[:, o + 2:o + 3], ALU.mult, ALU.mult)
                return v.tensor_scalar(stat[:, o + 5:o + 6], stat[:, o + 3:o + 4],
                                       1e-5, None, ALU.add)
            s3 = p.step("vector", stats3, after=[s2])
            s4 = p.step("vector",
                        lambda v, o=o: v.tensor_tensor(stat[:, o + 6:o + 7],
                                                       stat[:, o + 5:o + 6],
                                                       stat[:, o + 4:o + 5], ALU.subtract),
                        after=[s3])
            lns_st.append(s4)

            a1 = p.step("scalar",
                        lambda s, o=o: s.activation(stat[:, o + 7:o + 8],
                                                    stat[:, o + 6:o + 7], AF.Ln),
                        after=[s4])
            a2 = p.step("scalar",
                        lambda s, o=o: s.activation(stat[:, o + 8:o + 9],
                                                    stat[:, o + 7:o + 8], AF.Exp, scale=-0.5),
                        after=[a1])
            a3 = p.step("scalar",
                        lambda s, o=o: s.mul(stat[:, o + 9:o + 10], stat[:, o + 2:o + 3],
                                             stat[:, o + 8:o + 9]),
                        after=[a2])
            a4 = p.step("scalar",
                        lambda s, o=o: s.mul(stat[:, o + 10:o + 11], stat[:, o + 9:o + 10],
                                             -1.0),
                        after=[a3])

            def apply(s, i=i, o=o, b=b):
                return s.activation(xlnb[:, b, :], xdb[:, b, :], AF.Identity,
                                    bias=stat[:, o + 10:o + 11], scale=stat[:, o + 8:o + 9])
            adep = [a4] + ([rope_s[i - 2]] if i >= 2 else [])
            lns.append(p.step("scalar", apply, after=adep))

            def rope1(v, i=i, b=b):
                v.tensor_tensor(rx[:, b, 0:512], xlnb[:, b, 0:512], rcd[:, b, :], ALU.mult)
                v.tensor_tensor(rx[:, b, 512:1024], xlnb[:, b, 0:512], rsd[:, b, :], ALU.mult)
                v.scalar_tensor_tensor(scr[:, 0:512], xlnb[:, b, 512:1024], -1.0,
                                       rsd[:, b, :], ALU.mult, ALU.mult)
                return v.tensor_tensor(scr[:, 512:1024], xlnb[:, b, 512:1024],
                                       rcd[:, b, :], ALU.mult)
            rdep = [lns[i], lrc[i], lrs[i]] + ([tr_s[i - 2]] if i >= 2 else [])
            r1 = p.step("vector", rope1, after=rdep)

            def rope2(v, i=i, b=b):
                v.tensor_tensor(rx[:, b, 0:512], rx[:, b, 0:512], scr[:, 0:512], ALU.add)
                return v.tensor_tensor(rx[:, b, 512:1024], rx[:, b, 512:1024],
                                       scr[:, 512:1024], ALU.add)
            rope_s.append(p.step("vector", rope2, after=[r1]))

            def trf(t, i=i, b=b):
                half = b * NE * 128
                for ch in range(NE):
                    ins = t.transpose(psB[:, half + ch * 128: half + (ch + 1) * 128],
                                      rx[:, b, ch * 128:(ch + 1) * 128], idn[:, :])
                return ins
            tdep = [rope_s[i], ld["idn"]] + ([trs[i - 2]] if i >= 2 else [])
            tr_s.append(p.step("tensor", trf, after=tdep))

            def ev(v, i=i, b=b):
                half = b * NE * 128
                return v.tensor_copy(rxT[:, 0:NE, i * 128:(i + 1) * 128],
                                     psB[:, half:half + NE * 128]
                                     .rearrange("p (c q) -> p c q", q=128))
            trs.append(p.step("vector", ev, after=[tr_s[i]]))

        wimg0 = wb0[:, :, :]
        qmm, qev = emit_projT(p, wimg0, rxT, QT, NE, bias_col=bq_s,
                              after=[l_w0, l_bq, trs[-1]])
        l_wk = p.dma("gpsimd", wb0[:, :, :], wkd, after=[qmm[-1]])
        kmm, kev = emit_projT(p, wimg0, rxT, KT, NE, bias_col=bk_s,
                              after=[l_wk, l_bk, qev[-1]])
        l_wv = p.dma("gpsimd", wb0[:, :, :], wvd, after=[kmm[-1]])

        vsteps = []
        for i in range(NT):
            dep = [l_wv, trs[i], kev[-1]]
            if i >= 1:
                dep.append(vsteps[-1])

            def vmm(t, i=i):
                for fc in range(2):
                    for ec in range(NE):
                        ins = t.matmul(p.sb["psC"][:, fc * 512:(fc + 1) * 512],
                                       rxT[:, ec, i * 128:(i + 1) * 128],
                                       wb0[:, ec, fc * 512:(fc + 1) * 512],
                                       start=(ec == 0), stop=(ec == NE - 1))
                return ins
            m = p.step("tensor", vmm, after=dep)
            vsteps.append(p.step("vector",
                                 lambda v, i=i: v.tensor_copy(Vn[:, i, :],
                                                              p.sb["psC"][:, 0:E]),
                                 after=[m]))

        def bop_mm(t):
            for fc in range(2):
                for ec in range(NE):
                    ins = t.matmul(p.sb["psC"][0:1, fc * 512:(fc + 1) * 512],
                                   bv_b[:, ec:ec + 1],
                                   wb1[:, ec, fc * 512:(fc + 1) * 512],
                                   start=(ec == 0), stop=(ec == NE - 1))
            return ins
        bmm = p.step("tensor", bop_mm, after=[l_wo, bvc, vsteps[-1]])
        bev = p.step("vector", lambda v: v.tensor_tensor(bop[:1, :],
                                                         p.sb["psC"][0:1, 0:E],
                                                         bo_s[:1, :], ALU.add),
                     after=[bmm, l_bo])

    OT = p.sbuf("OT", [128, NE, T], BF16)
    PT = p.sbuf("PT", [128, NT, 1024], BF16)
    xrs = p.sbuf("xrs", [128, E], F32)
    xon = p.sbuf("xon", [128, E], F32)

    def head_q(h, qh):
        return lambda qt: QT[(h % 2) * 64:(h % 2) * 64 + 64, h // 2,
                             (qh * 8 + qt) * 128:(qh * 8 + qt + 1) * 128]
    def head_k(h):
        return lambda k0, k1: KT[(h % 2) * 64:(h % 2) * 64 + 64, h // 2, k0:k1]
    def head_v(h):
        return lambda kc: Vn[:, kc, h * 64:(h + 1) * 64]

    last = bev
    for qh in range(2):
        for h in range(16):
            pv = emit_attn_head(p, head_q(h, qh), head_k(h), head_v(h), scale,
                                nqt=8, kmax_fn=lambda qt: T,
                                pv_psum="psC", pv_parts=64, pv_width=1024,
                                after=[qev[-1], kev[-1], vsteps[-1], last])

            def oevf(s, h=h, qh=qh):
                return s.activation(OT[(h % 2) * 64:(h % 2) * 64 + 64, h // 2,
                                       qh * 1024:(qh + 1) * 1024],
                                    p.sb["psC"][0:64, 0:1024], AF.Copy)
            last = p.step("scalar", oevf, after=[pv])

    xrl, adds, sts = [], [], []
    for i in range(NT):
        xrl.append(p.dma("sync", xrs[:, :], x_in[:, i * E:(i + 1) * E],
                         after=[adds[i - 1]] if i >= 1 else [bev]))

        def wmm(t, i=i):
            for fc in range(2):
                for ec in range(NE):
                    t.matmul(p.sb["psC"][:, fc * 512:(fc + 1) * 512],
                             OT[:, ec, i * 128:(i + 1) * 128],
                             wb1[:, ec, fc * 512:(fc + 1) * 512],
                             start=(ec == 0), stop=False)
                ins = t.matmul(p.sb["psC"][:, fc * 512:(fc + 1) * 512],
                               p.sb["ones_row"][:1, :],
                               bop[:1, fc * 512:(fc + 1) * 512],
                               start=False, stop=True)
            return ins
        dep = [last, bev]
        if i >= 1:
            dep.append(adds[i - 1])
        m = p.step("tensor", wmm, after=dep)

        def addf(v, i=i):
            return v.tensor_tensor(xon[:, :], p.sb["psC"][:, 0:E], xrs[:, :], ALU.add)
        a_dep = [m, xrl[i]]
        if i >= 1:
            a_dep.append(sts[i - 1])
        adds.append(p.step("vector", addf, after=a_dep))
        sts.append(p.dma("sync", x_out[:, i * E:(i + 1) * E], xon[:, :],
                         after=[adds[i]]))
    p.step("gpsimd", lambda g: g.engine_nop(), after=[sts[-1]])
    return p


def _l0b_host(inputs):
    d = {}
    d["wqT"] = tile_lhsT(np.asarray(inputs["L0_attn_b_Wq"]).T)
    d["wkT"] = tile_lhsT(np.asarray(inputs["L0_attn_b_Wk"]).T)
    d["wvT"] = tile_lhsT(np.asarray(inputs["L0_attn_b_Wv"]).T)
    d["woT"] = tile_lhsT(np.asarray(inputs["L0_attn_b_Wo"]).T)
    d["bq"] = col128(inputs["L0_attn_b_bq"])
    d["bk"] = col128(inputs["L0_attn_b_bk"])
    d["bv"] = col128(inputs["L0_attn_b_bv"])
    d["bo"] = np.asarray(inputs["L0_attn_b_bo"], dtype=np.float32).reshape(1, E)
    dd = 1024
    freq = np.exp(np.arange(0, dd, 2, dtype=np.float64) * (-np.log(10000.0) / dd))
    ang = np.arange(T, dtype=np.float64)[:, None] * freq[None, :]
    d["ropecos"] = nat_tiles(np.cos(ang).astype(np.float32))
    d["ropesin"] = nat_tiles(np.sin(ang).astype(np.float32))
    d["c_idn"] = np.eye(128, dtype=np.float32)
    return d


def pack_x(x):
    return np.stack([nat_tiles(np.asarray(x[b], dtype=np.float32)) for b in range(x.shape[0])])


def unpack_x(xp):
    n = xp.shape[0]
    out = np.empty((n, T, E), dtype=np.float32)
    for b in range(n):
        out[b] = xp[b].reshape(128, NT, E).transpose(1, 0, 2).reshape(T, E)
    return out


_PROG_CACHE = {}

def _run(build_fn, key, shared, x_cores, extra_per_core=None, n_cores=8):
    if key not in _PROG_CACHE:
        _PROG_CACHE[key] = build_fn().build()
    nc = _PROG_CACHE[key]
    in_maps = []
    for b in range(n_cores):
        m = dict(shared)
        m["x"] = x_cores[b]
        if extra_per_core is not None:
            m.update(extra_per_core[b])
        in_maps.append(m)
    res = run_bass_kernel_spmd(nc, in_maps, core_ids=list(range(n_cores)))
    return np.stack([r["xo"] for r in res.results])


def kernel(**inputs):
    x = pack_x(np.asarray(inputs["L0_attn_a_x"], dtype=np.float32))
    h_l0a = _l0a_host(inputs)
    h_l0f = _l0f_host(inputs)
    h_l0b = _l0b_host(inputs)
    h_l1a = _l1a_host(inputs)
    h_l1f = _l1f_host(inputs)
    h_l1b = _l1b_host(inputs)
    x = _run(build_l0_attn_a, "l0a", h_l0a, x)
    x = _run(build_l0_ffn, "l0f", h_l0f, x)
    x = _run(build_l0_attn_b, "l0b", h_l0b, x)
    x = _run(build_l0_ffn, "l0f", h_l0f, x)
    x = _run(build_l1_attn_a, "l1a", h_l1a, x)
    x = _run(build_l1_ffn, "l1f", h_l1f, x)
    x = _run(build_l1_attn_b, "l1b", h_l1b, x)
    x = _run(build_l1_ffn, "l1f", h_l1f, x)
    return unpack_x(x)


def emit_ln_T(p, x_in, xdb, xlnb, xT, extra_ln_out=None):
    stat, scr = p.sb["stat"], p.sb["scr"]
    psB, idn = p.sb["psB"], p.sb["idn"]
    xls, lns, tr_s, trs = [], [], [], []
    for i in range(NT):
        b = i % 2
        o = b * 12
        xls.append(p.dma("sync", xdb[:, b, :], x_in[:, i * E:(i + 1) * E],
                         after=[lns[i - 2]] if i >= 2 else []))

        def stats1(v, i=i, o=o, b=b):
            v.scalar_tensor_tensor(scr[:, 0:E], xdb[:, b, :], 1.0, xdb[:, b, :],
                                   ALU.mult, ALU.mult, accum_out=stat[:, o + 1:o + 2])
            return v.tensor_reduce(stat[:, o:o + 1], xdb[:, b, :], AX.X, ALU.add)
        s1 = p.step("vector", stats1, after=[xls[i]] + ([lns[i - 2]] if i >= 2 else []))

        def stats2(v, i=i, o=o):
            v.tensor_scalar(stat[:, o + 2:o + 3], stat[:, o:o + 1], 1.0 / E, None, ALU.mult)
            return v.tensor_scalar(stat[:, o + 3:o + 4], stat[:, o + 1:o + 2],
                                   1.0 / E, None, ALU.mult)
        s2 = p.step("vector", stats2, after=[s1])

        def stats3(v, i=i, o=o):
            v.scalar_tensor_tensor(stat[:, o + 4:o + 5], stat[:, o + 2:o + 3], 1.0,
                                   stat[:, o + 2:o + 3], ALU.mult, ALU.mult)
            return v.tensor_scalar(stat[:, o + 5:o + 6], stat[:, o + 3:o + 4],
                                   1e-5, None, ALU.add)
        s3 = p.step("vector", stats3, after=[s2])
        s4 = p.step("vector",
                    lambda v, o=o: v.tensor_tensor(stat[:, o + 6:o + 7],
                                                   stat[:, o + 5:o + 6],
                                                   stat[:, o + 4:o + 5], ALU.subtract),
                    after=[s3])
        a1 = p.step("scalar",
                    lambda s, o=o: s.activation(stat[:, o + 7:o + 8],
                                                stat[:, o + 6:o + 7], AF.Ln),
                    after=[s4])
        a2 = p.step("scalar",
                    lambda s, o=o: s.activation(stat[:, o + 8:o + 9],
                                                stat[:, o + 7:o + 8], AF.Exp, scale=-0.5),
                    after=[a1])
        a3 = p.step("scalar",
                    lambda s, o=o: s.mul(stat[:, o + 9:o + 10], stat[:, o + 2:o + 3],
                                         stat[:, o + 8:o + 9]),
                    after=[a2])
        a4 = p.step("scalar",
                    lambda s, o=o: s.mul(stat[:, o + 10:o + 11], stat[:, o + 9:o + 10],
                                         -1.0),
                    after=[a3])

        def apply(s, i=i, o=o, b=b):
            return s.activation(xlnb[:, b, :], xdb[:, b, :], AF.Identity,
                                bias=stat[:, o + 10:o + 11], scale=stat[:, o + 8:o + 9])
        adep = [a4] + ([tr_s[i - 2]] if i >= 2 else [])
        lns.append(p.step("scalar", apply, after=adep))

        def trf(t, i=i, b=b):
            half = b * NE * 128
            for ch in range(NE):
                ins = t.transpose(psB[:, half + ch * 128: half + (ch + 1) * 128],
                                  xlnb[:, b, ch * 128:(ch + 1) * 128], idn[:, :])
            return ins
        tdep = [lns[i]] + ([trs[i - 2]] if i >= 2 else [])
        tr_s.append(p.step("tensor", trf, after=tdep))

        def ev(v, i=i, b=b):
            half = b * NE * 128
            return v.tensor_copy(xT[:, 0:NE, i * 128:(i + 1) * 128],
                                 psB[:, half:half + NE * 128]
                                 .rearrange("p (c q) -> p c q", q=128))
        trs.append(p.step("vector", ev, after=[tr_s[i]]))
    return xls, lns, trs


def emit_out_residual(p, OT, w_img, x_in, x_out, after, aug_row=None, nec=NE):
    xrs, xon = p.sb["xrs"], p.sb["xon"]
    psC = p.sb["psC"]
    xrl, adds, sts = [], [], []
    for i in range(NT):
        xrl.append(p.dma("sync", xrs[:, :], x_in[:, i * E:(i + 1) * E],
                         after=[adds[i - 1]] if i >= 1 else list(after)))

        def wmm(t, i=i):
            for fc in range(2):
                for ec in range(nec):
                    ins = t.matmul(psC[:, fc * 512:(fc + 1) * 512],
                                   OT[:, ec, i * 128:(i + 1) * 128],
                                   w_img[:, ec, fc * 512:(fc + 1) * 512],
                                   start=(ec == 0), stop=(aug_row is None and ec == nec - 1))
                if aug_row is not None:
                    ones_r, bias_r = aug_row
                    ins = t.matmul(psC[:, fc * 512:(fc + 1) * 512], ones_r,
                                   bias_r[:1, fc * 512:(fc + 1) * 512],
                                   start=False, stop=True)
            return ins
        dep = list(after)
        if i >= 1:
            dep.append(adds[i - 1])
        m = p.step("tensor", wmm, after=dep)

        def addf(v, i=i):
            return v.tensor_tensor(xon[:, :], psC[:, 0:E], xrs[:, :], ALU.add)
        a_dep = [m, xrl[i]]
        if i >= 1:
            a_dep.append(sts[i - 1])
        adds.append(p.step("vector", addf, after=a_dep))
        sts.append(p.dma("sync", x_out[:, i * E:(i + 1) * E], xon[:, :],
                         after=[adds[i]]))
    return sts[-1]



def build_l1_attn_b():
    scale = 1.0 / 8.0
    p = Prog("l1b")
    nc = p.nc
    x_in = nc.dram_tensor("x", [128, NT * E], F32, kind="ExternalInput").ap()
    wqd = nc.dram_tensor("wqT", [128, NE * E], F32, kind="ExternalInput").ap()
    wkd = nc.dram_tensor("wkT", [128, NE * E], F32, kind="ExternalInput").ap()
    wvd = nc.dram_tensor("wvT", [128, NE * E], F32, kind="ExternalInput").ap()
    wpd = nc.dram_tensor("wpT", [128, NE * E], F32, kind="ExternalInput").ap()
    bpd = nc.dram_tensor("bp", [1, E], F32, kind="ExternalInput").ap()
    x_out = nc.dram_tensor("xo", [128, NT * E], F32, kind="ExternalOutput").ap()

    ld = alloc_common(p, need_masks=("mlow",))
    QT = p.sbuf("QT", [128, NE, T], BF16)
    KT = p.sbuf("KT", [128, NE, T], BF16)
    Vn = p.sbuf("Vn", [128, NT, E], BF16)
    wb1 = p.sbuf("wb1", [128, NE, E], BF16)
    bp_s = p.sbuf("bp_s", [1, E], BF16)
    l_bp = p.dma("gpsimd", bp_s[:1, :], bpd)
    l_wp = p.dma("gpsimd", wb1[:, :, :], wpd)

    with ExitStack() as lv1:
        e1 = lv1.enter_context
        xdb = e1(nc.sbuf_tensor("xdb", [128, 2, E], F32))
        xlnb = e1(nc.sbuf_tensor("xlnb", [128, 2, E], BF16))
        xT = e1(nc.sbuf_tensor("xT", [128, NE, T], BF16))
        wb0 = e1(nc.sbuf_tensor("wb0", [128, NE, E], BF16))

        l_w0 = p.dma("gpsimd", wb0[:, :, :], wqd)
        xls, lns, trs = emit_ln_T(p, x_in, xdb, xlnb, xT)

        qmm, qev = emit_projT(p, wb0[:, :, :], xT, QT, NE, after=[l_w0, trs[-1]])
        l_wk = p.dma("gpsimd", wb0[:, :, :], wkd, after=[qmm[-1]])
        kmm, kev = emit_projT(p, wb0[:, :, :], xT, KT, NE, after=[l_wk, qev[-1]])
        l_wv = p.dma("gpsimd", wb0[:, :, :], wvd, after=[kmm[-1]])

        Pn = p.sb["Pn"]
        psA, psB, idn = p.sb["psA"], p.sb["psB"], p.sb["idn"]
        last_ev = kev[-1]
        vdone = []
        for ob in range(NE):
            def vmm(t, ob=ob):
                for th in range(4):
                    for ec in range(NE):
                        ins = t.matmul(psA[:, th * 512:(th + 1) * 512],
                                       wb0[:, ec, ob * 128:(ob + 1) * 128],
                                       xT[:, ec, th * 512:(th + 1) * 512],
                                       start=(ec == 0), stop=(ec == NE - 1))
                return ins
            m = p.step("tensor", vmm, after=[l_wv, last_ev, trs[-1]])
            sev = p.step("scalar", lambda s, ob=ob: s.activation(Pn[:, 0, :], psA[:, :],
                                                                 AF.Copy), after=[m])

            def vtr(t, ob=ob):
                for i in range(NT):
                    ins = t.transpose(psB[:, i * 128:(i + 1) * 128],
                                      Pn[:, 0, i * 128:(i + 1) * 128], idn[:, :])
                return ins
            tstep = p.step("tensor", vtr, after=[sev])
            vev = p.step("vector",
                         lambda v, ob=ob: v.tensor_copy(
                             Vn[:, 0:NT, ob * 128:(ob + 1) * 128],
                             psB[:, :].rearrange("p (c q) -> p c q", q=128)),
                         after=[tstep])
            last_ev = vev
            vdone.append(vev)

    OT = p.sbuf("OT", [128, NE, T], BF16)
    PT = p.sbuf("PT", [128, NT, 1024], BF16)
    xrs = p.sbuf("xrs", [128, E], F32)
    xon = p.sbuf("xon", [128, E], F32)

    def head_q(h, qh):
        return lambda qt: QT[(h % 2) * 64:(h % 2) * 64 + 64, h // 2,
                             (qh * 8 + qt) * 128:(qh * 8 + qt + 1) * 128]
    def head_k(h):
        return lambda k0, k1: KT[(h % 2) * 64:(h % 2) * 64 + 64, h // 2, k0:k1]
    def head_v(h):
        return lambda kc: Vn[:, kc, h * 64:(h + 1) * 64]

    last = vdone[-1]
    for qh in range(2):
        for h in range(16):
            qoff = qh * 8
            pv = emit_attn_head(
                p, head_q(h, qh), head_k(h), head_v(h), scale,
                nqt=8, kmax_fn=lambda qt, qoff=qoff: (qoff + qt + 1) * 128,
                causal=True,
                mask_fn=lambda qt, qoff=qoff: (p.sb["mlow"][:, :], (qoff + qt) * 128),
                pv_psum="psC", pv_parts=64, pv_width=1024,
                after=[qev[-1], kev[-1], vdone[-1], last])

            def oevf(s, h=h, qh=qh):
                return s.activation(OT[(h % 2) * 64:(h % 2) * 64 + 64, h // 2,
                                       qh * 1024:(qh + 1) * 1024],
                                    p.sb["psC"][0:64, 0:1024], AF.Copy)
            last = p.step("scalar", oevf, after=[pv])

    st_last = emit_out_residual(p, OT, wb1, x_in, x_out, [last, l_wp, l_bp],
                                aug_row=(p.sb["ones_row"][:1, :], bp_s))
    p.step("gpsimd", lambda g: g.engine_nop(), after=[st_last])
    return p


def _l1b_host(inputs):
    d = {}
    wqkv = np.asarray(inputs["L1_attn_b_w_qkv"], dtype=np.float32)
    Wq = wqkv[:, 0:64, :].reshape(1024, 1024)
    Wk = wqkv[:, 64:128, :].reshape(1024, 1024)
    Wv = wqkv[:, 128:192, :].reshape(1024, 1024)
    d["wqT"] = tile_lhsT(Wq.T)
    d["wkT"] = tile_lhsT(Wk.T)
    d["wvT"] = tile_lhsT(Wv.T)
    d["wpT"] = tile_lhsT(np.asarray(inputs["L1_attn_b_w_proj"]).T)
    d["bp"] = np.asarray(inputs["L1_attn_b_b_proj"], dtype=np.float32).reshape(1, E)
    d["c_idn"] = np.eye(128, dtype=np.float32)
    r = np.arange(128)
    d["c_mlow"] = np.where(r[:, None] >= r[None, :], 0.0, NEG).astype(np.float32)
    return d


LAMBDA_INIT = 0.8 - 0.6 * float(np.exp(-0.3))

def build_l0_attn_a():
    p = Prog("l0a")
    nc = p.nc
    x_in = nc.dram_tensor("x", [128, NT * E], F32, kind="ExternalInput").ap()
    wqd = nc.dram_tensor("wqT", [128, NE * E], F32, kind="ExternalInput").ap()
    wkd = nc.dram_tensor("wkT", [128, NE * E], F32, kind="ExternalInput").ap()
    wvd = nc.dram_tensor("wvT", [128, NE * E], F32, kind="ExternalInput").ap()
    wod = nc.dram_tensor("woT", [128, NE * E], F32, kind="ExternalInput").ap()
    lamd = nc.dram_tensor("lams", [1, 256], F32, kind="ExternalInput").ap()
    aqd = nc.dram_tensor("ropeAq", [128, NT * 64], F32, kind="ExternalInput").ap()
    bqd = nc.dram_tensor("ropeBq", [128, NT * 64], F32, kind="ExternalInput").ap()
    akd = nc.dram_tensor("ropeAk", [128, NT * 64], F32, kind="ExternalInput").ap()
    bkd = nc.dram_tensor("ropeBk", [128, NT * 64], F32, kind="ExternalInput").ap()
    x_out = nc.dram_tensor("xo", [128, NT * E], F32, kind="ExternalOutput").ap()

    ld = alloc_common(p)
    QT = p.sbuf("QT", [128, NE, T], BF16)
    KT = p.sbuf("KT", [128, NE, T], BF16)
    Vn = p.sbuf("Vn", [128, NT, E], BF16)
    wb1 = p.sbuf("wb1", [128, NE, E], BF16)
    nlc = p.sbuf("nlc", [128, 1], F32)
    onesf = p.sbuf("onesf", [1, 128], F32)
    onesc = p.sbuf("onesc", [128, 1], F32)
    lam_s = p.sbuf("lam_s", [1, 260], F32)
    l_lam = p.dma("gpsimd", lam_s[:1, 0:256], lamd)
    l_wo = p.dma("gpsimd", wb1[:, :, :], wod)
    ofm = p.step("vector", lambda v: v.memset(onesf[:1, :], 1.0))
    ocm = p.step("vector", lambda v: v.memset(onesc[:, :], 1.0))

    lm1 = p.step("vector", lambda v: v.tensor_tensor(lam_s[:1, 0:64], lam_s[:1, 0:64],
                                                     lam_s[:1, 64:128], ALU.mult),
                 after=[l_lam])
    lm2 = p.step("vector", lambda v: v.tensor_tensor(lam_s[:1, 128:192], lam_s[:1, 128:192],
                                                     lam_s[:1, 192:256], ALU.mult),
                 after=[lm1])
    lr1 = p.step("vector", lambda v: v.tensor_reduce(lam_s[:1, 256:257], lam_s[:1, 0:64],
                                                     AX.X, ALU.add), after=[lm2])
    lr2 = p.step("vector", lambda v: v.tensor_reduce(lam_s[:1, 257:258], lam_s[:1, 128:192],
                                                     AX.X, ALU.add), after=[lr1])
    le1 = p.step("scalar", lambda s: s.activation(lam_s[:1, 256:257], lam_s[:1, 256:257],
                                                  AF.Exp), after=[lr2])
    le2 = p.step("scalar", lambda s: s.activation(lam_s[:1, 257:258], lam_s[:1, 257:258],
                                                  AF.Exp), after=[le1])
    lc1 = p.step("vector", lambda v: v.tensor_tensor(lam_s[:1, 258:259], lam_s[:1, 257:258],
                                                     lam_s[:1, 256:257], ALU.subtract),
                 after=[le2])
    lc2 = p.step("vector", lambda v: v.tensor_scalar(lam_s[:1, 259:260], lam_s[:1, 258:259],
                                                     -LAMBDA_INIT, None, ALU.add),
                 after=[lc1])
    lbc = p.step("tensor", lambda t: t.matmul(p.sb["psC"][:, 0:1], onesf[:1, :],
                                              lam_s[:1, 259:260], start=True, stop=True),
                 after=[lc2, ofm])
    lev = p.step("vector", lambda v: v.tensor_copy(nlc[:, :], p.sb["psC"][:, 0:1]),
                 after=[lbc])

    with ExitStack() as lv1:
        e1 = lv1.enter_context
        xdb = e1(nc.sbuf_tensor("xdb", [128, 2, E], F32))
        xlnb = e1(nc.sbuf_tensor("xlnb", [128, 2, E], BF16))
        xT = e1(nc.sbuf_tensor("xT", [128, NE, T], BF16))
        wb0 = e1(nc.sbuf_tensor("wb0", [128, NE, E], BF16))
        qn = e1(nc.sbuf_tensor("qn", [128, 2, E], BF16))
        qr = e1(nc.sbuf_tensor("qr", [128, 2, E], BF16))
        Aq = e1(nc.sbuf_tensor("Aq", [128, NT, 64], BF16))
        Bq = e1(nc.sbuf_tensor("Bq", [128, NT, 64], BF16))
        Ak = e1(nc.sbuf_tensor("Ak", [128, NT, 64], BF16))
        Bk = e1(nc.sbuf_tensor("Bk", [128, NT, 64], BF16))
        l_tabs = [p.dma("gpsimd", Aq[:, :, :], aqd), p.dma("gpsimd", Bq[:, :, :], bqd),
                  p.dma("gpsimd", Ak[:, :, :], akd), p.dma("gpsimd", Bk[:, :, :], bkd)]

        l_w0 = p.dma("gpsimd", wb0[:, :, :], wqd)
        xls, lns, trs = emit_ln_T(p, x_in, xdb, xlnb, xT)

        psB, psC, idn, scr = p.sb["psB"], p.sb["psC"], p.sb["idn"], p.sb["scr"]

        def proj_rope(dst, Atab, Btab, l_w, prev_last):
            evs = []
            last = prev_last
            for i in range(NT):
                b = i % 2

                def mm(t, i=i):
                    for fc in range(2):
                        for ec in range(NE):
                            ins = t.matmul(psC[:, fc * 512:(fc + 1) * 512],
                                           xT[:, ec, i * 128:(i + 1) * 128],
                                           wb0[:, ec, fc * 512:(fc + 1) * 512],
                                           start=(ec == 0), stop=(ec == NE - 1))
                    return ins
                dep = [l_w, trs[-1], trs[-2]]
                if i >= 1:
                    dep.append(evs[-1][0])
                else:
                    dep.append(prev_last)
                m = p.step("tensor", mm, after=dep)
                ev = p.step("vector", lambda v, i=i, b=b: v.tensor_copy(qn[:, b, :],
                                                                        psC[:, 0:E]),
                            after=[m] + ([evs[-2][1], evs[-2][2]] if i >= 2 else []))

                def r1(v, i=i, b=b):
                    return v.tensor_tensor(
                        qr[:, b, :].rearrange("p (h d) -> p h d", d=64),
                        qn[:, b, :].rearrange("p (h d) -> p h d", d=64),
                        Atab[:, i, :].unsqueeze(1).to_broadcast([128, 16, 64]),
                        ALU.mult)
                s1 = p.step("vector", r1, after=[ev, l_tabs[0], l_tabs[1],
                                                 l_tabs[2], l_tabs[3]] +
                            ([evs[-2][3]] if i >= 2 else []))

                def r2(v, i=i, b=b):
                    sw = qn[:, b, :].rearrange("p (h q two) -> p h q two", two=2, q=32)
                    return v.tensor_tensor(
                        scr[:, 0:E].rearrange("p (h q two) -> p h q two", two=2, q=32),
                        sw[:, :, :, ::-1],
                        Btab[:, i, :].rearrange("p (q two) -> p q two", two=2)
                        .unsqueeze(1).to_broadcast([128, 16, 32, 2]),
                        ALU.mult)
                s2 = p.step("vector", r2, after=[ev])

                def r3(v, i=i, b=b):
                    return v.tensor_tensor(qr[:, b, :], qr[:, b, :], scr[:, 0:E], ALU.add)
                s3 = p.step("vector", r3, after=[s1, s2])

                def trf(t, i=i, b=b):
                    half = b * NE * 128
                    for ch in range(NE):
                        ins = t.transpose(psB[:, half + ch * 128: half + (ch + 1) * 128],
                                          qr[:, b, ch * 128:(ch + 1) * 128], idn[:, :])
                    return ins
                t1 = p.step("tensor", trf, after=[s3] + ([evs[-2][4]] if i >= 2 else []))

                def evt(v, i=i, b=b):
                    half = b * NE * 128
                    return v.tensor_copy(dst[:, 0:NE, i * 128:(i + 1) * 128],
                                         psB[:, half:half + NE * 128]
                                         .rearrange("p (c q) -> p c q", q=128))
                e2 = p.step("vector", evt, after=[t1])
                evs.append((ev, s1, s2, t1, e2))
                last = e2
            return last

        qlast = proj_rope(QT, Aq, Bq, l_w0, lev)
        l_wk = p.dma("gpsimd", wb0[:, :, :], wkd, after=[qlast])
        klast = proj_rope(KT, Ak, Bk, l_wk, qlast)
        l_wv = p.dma("gpsimd", wb0[:, :, :], wvd, after=[klast])

        vsteps = []
        for i in range(NT):
            def vmm(t, i=i):
                for fc in range(2):
                    for ec in range(NE):
                        ins = t.matmul(psC[:, fc * 512:(fc + 1) * 512],
                                       xT[:, ec, i * 128:(i + 1) * 128],
                                       wb0[:, ec, fc * 512:(fc + 1) * 512],
                                       start=(ec == 0), stop=(ec == NE - 1))
                return ins
            dep = [l_wv, klast]
            if i >= 1:
                dep.append(vsteps[-1])
            m = p.step("tensor", vmm, after=dep)
            vsteps.append(p.step("vector",
                                 lambda v, i=i: v.tensor_copy(Vn[:, i, :], psC[:, 0:E]),
                                 after=[m]))

    OT = p.sbuf("OT", [128, NE, T], BF16)
    PT = p.sbuf("PT", [128, NT, 1024], BF16)
    xrs = p.sbuf("xrs", [128, E], F32)
    xon = p.sbuf("xon", [128, E], F32)
    psA, psC, scr = p.sb["psA"], p.sb["psC"], p.sb["scr"]

    def hh_q(hh, qh):
        return lambda qt: QT[(hh % 2) * 64:(hh % 2) * 64 + 64, hh // 2,
                             (qh * 8 + qt) * 128:(qh * 8 + qt + 1) * 128]
    def hh_k(hh):
        return lambda k0, k1: KT[(hh % 2) * 64:(hh % 2) * 64 + 64, hh // 2, k0:k1]
    def pair_v(h):
        return lambda kc: Vn[:, kc, h * 128:(h + 1) * 128]

    LN1MLI = float(np.log(1.0 - LAMBDA_INIT))
    last = vsteps[-1]
    for qh in range(2):
        for h in range(8):
            pv0 = emit_attn_head(p, hh_q(2 * h, qh), hh_k(2 * h), pair_v(h), 1.0,
                                 nqt=8, kmax_fn=lambda qt: T,
                                 pv_psum="psC", pv_parts=128, pv_width=1024,
                                 pv_start=True, pv_stop=False,
                                 after=[qlast, klast, vsteps[-1], last, lev])
            pv1 = emit_attn_head(p, hh_q(2 * h + 1, qh), hh_k(2 * h + 1), pair_v(h), 1.0,
                                 nqt=8, kmax_fn=lambda qt: T,
                                 diag_extra=nlc[:, 0:1],
                                 pv_psum="psC", pv_parts=128, pv_width=1024,
                                 pv_start=False, pv_stop=True,
                                 after=[pv0, lev])
            sq = p.step("scalar", lambda s: s.activation(scr[:, 0:1024], psC[:, 0:1024],
                                                         AF.Square), after=[pv1])
            def smf(t):
                for fc in range(2):
                    ins = t.matmul(psA[0:1, fc * 512:(fc + 1) * 512], onesc[:, 0:1],
                                   scr[:, fc * 512:(fc + 1) * 512], start=True, stop=True)
                return ins
            sm = p.step("tensor", smf, after=[sq, ocm])
            r0 = p.step("vector", lambda v: v.tensor_scalar(psA[0:1, 1024:2048],
                                                            psA[0:1, 0:1024],
                                                            1.0 / 128, 1e-5,
                                                            ALU.mult, ALU.add),
                        after=[sm])
            r1 = p.step("scalar", lambda s: s.activation(psA[0:1, 1024:2048],
                                                         psA[0:1, 1024:2048], AF.Ln),
                        after=[r0])
            r2 = p.step("scalar", lambda s: s.activation(scr[0:1, 0:1024],
                                                         psA[0:1, 1024:2048], AF.Exp,
                                                         scale=-0.5),
                        after=[r1])
            def bcf(t):
                for fc in range(2):
                    ins = t.matmul(psA[:, fc * 512:(fc + 1) * 512], onesf[:1, :],
                                   scr[0:1, fc * 512:(fc + 1) * 512], start=True,
                                   stop=True, skip_group_check=True)
                return ins
            bc = p.step("tensor", bcf, after=[r2, ofm])
            bev = p.step("vector", lambda v: v.tensor_copy(scr[:, 0:1024],
                                                           psA[:, 0:1024]), after=[bc])
            nm = p.step("vector",
                        lambda v, h=h, qh=qh: v.scalar_tensor_tensor(
                            OT[:, h, qh * 1024:(qh + 1) * 1024],
                            psC[:, 0:1024], 1.0 - LAMBDA_INIT, scr[:, 0:1024],
                            ALU.mult, ALU.mult),
                        after=[bev])
            last = nm

    st_last = emit_out_residual(p, OT, wb1, x_in, x_out, [last, l_wo])
    p.step("gpsimd", lambda g: g.engine_nop(), after=[st_last])
    return p


def _l0a_host(inputs):
    d = {}
    d["wqT"] = tile_lhsT(np.asarray(inputs["L0_attn_a_Wq"]).T)
    d["wkT"] = tile_lhsT(np.asarray(inputs["L0_attn_a_Wk"]).T)
    d["wvT"] = tile_lhsT(np.asarray(inputs["L0_attn_a_Wv"]).T)
    d["woT"] = tile_lhsT(np.asarray(inputs["L0_attn_a_Wo"]).T)
    d["lams"] = np.concatenate([
        np.asarray(inputs["L0_attn_a_lambda_q1"]), np.asarray(inputs["L0_attn_a_lambda_k1"]),
        np.asarray(inputs["L0_attn_a_lambda_q2"]), np.asarray(inputs["L0_attn_a_lambda_k2"]),
    ]).astype(np.float32).reshape(1, 256)
    Dh = 64
    inv = 1.0 / (10000.0 ** (np.arange(0, Dh, 2, dtype=np.float64) / Dh))
    pos = np.arange(T, dtype=np.float64)
    fr = pos[:, None] * inv[None, :]
    freqs = np.repeat(fr, 2, axis=-1)
    sv = (np.arange(0, Dh, 2, dtype=np.float64) + 0.4 * Dh) / (1.4 * Dh)
    power = (pos - T // 2) / 512.0
    sc_half = sv[None, :] ** power[:, None]
    scale = np.concatenate([sc_half, sc_half], axis=-1)
    sgn = np.tile(np.array([-1.0, 1.0]), Dh // 2)[None, :]
    scaling = Dh ** -0.5
    d["ropeAq"] = nat_tiles((np.cos(freqs) * scale * scaling).astype(np.float32))
    d["ropeBq"] = nat_tiles((np.sin(freqs) * scale * sgn * scaling).astype(np.float32))
    d["ropeAk"] = nat_tiles((np.cos(freqs) / scale).astype(np.float32))
    d["ropeBk"] = nat_tiles((np.sin(freqs) / scale * sgn).astype(np.float32))
    d["c_idn"] = np.eye(128, dtype=np.float32)
    return d


def build_l1_attn_a():
    p = Prog("l1a")
    nc = p.nc
    x_in = nc.dram_tensor("x", [128, NT * E], F32, kind="ExternalInput").ap()
    wqd = nc.dram_tensor("wqT", [128, NE * E], F32, kind="ExternalInput").ap()
    wkd = nc.dram_tensor("wkT", [128, NE * E], F32, kind="ExternalInput").ap()
    wvd = nc.dram_tensor("wvT", [128, NE * E], F32, kind="ExternalInput").ap()
    wod = nc.dram_tensor("woT", [128, NE * E], F32, kind="ExternalInput").ap()
    bqd = nc.dram_tensor("bq", [128, NE], F32, kind="ExternalInput").ap()
    bkd = nc.dram_tensor("bk", [128, NE], F32, kind="ExternalInput").ap()
    bvd = nc.dram_tensor("bv", [128, NE], F32, kind="ExternalInput").ap()
    bod = nc.dram_tensor("bo", [1, E], F32, kind="ExternalInput").ap()
    mupd = nc.dram_tensor("c_mup01", [128, 128], F32, kind="ExternalInput").ap()
    x_out = nc.dram_tensor("xo", [128, NT * E], F32, kind="ExternalInput" if False else "ExternalOutput").ap()

    ld = alloc_common(p)
    QT = p.sbuf("QT", [128, NE, T], BF16)
    KT = p.sbuf("KT", [128, NE, T], BF16)
    Vn = p.sbuf("Vn", [128, NT, E], BF16)
    wb1 = p.sbuf("wb1", [128, NE, E], BF16)
    bq_s = p.sbuf("bq_s", [128, NE], F32)
    bk_s = p.sbuf("bk_s", [128, NE], F32)
    bv_s = p.sbuf("bv_s", [128, NE], F32)
    bo_s = p.sbuf("bo_s", [1, E], BF16)
    mup01 = p.sbuf("mup01", [128, 128], BF16)
    Knc = p.sbuf("Knc", [128, 2, 64], BF16)
    st_sb = p.sbuf("st_sb", [128, 64], BF16)

    l_bq = p.dma("gpsimd", bq_s[:, :], bqd)
    l_bk = p.dma("gpsimd", bk_s[:, :], bkd)
    l_bv = p.dma("gpsimd", bv_s[:, :], bvd)
    l_bo = p.dma("gpsimd", bo_s[:1, :], bod)
    l_mu = p.dma("gpsimd", mup01[:, :], mupd)
    l_wo = p.dma("gpsimd", wb1[:, :, :], wod)
    psA, psB, psC = p.sb["psA"], p.sb["psB"], p.sb["psC"]
    Pn, Pn2, idn, scr = p.sb["Pn"], p.sb["Pn2"], p.sb["idn"], p.sb["scr"]

    with ExitStack() as lv1:
        e1 = lv1.enter_context
        xdb = e1(nc.sbuf_tensor("xdb", [128, 2, E], F32))
        xlnb = e1(nc.sbuf_tensor("xlnb", [128, 2, E], BF16))
        xT = e1(nc.sbuf_tensor("xT", [128, NE, T], BF16))
        wb0 = e1(nc.sbuf_tensor("wb0", [128, NE, E], BF16))

        l_w0 = p.dma("gpsimd", wb0[:, :, :], wqd)
        xls, lns, trs = emit_ln_T(p, x_in, xdb, xlnb, xT)

        qmm, qev = emit_projT(p, wb0[:, :, :], xT, QT, NE, bias_col=bq_s,
                              after=[l_w0, l_bq, trs[-1]])
        l_wk = p.dma("gpsimd", wb0[:, :, :], wkd, after=[qmm[-1]])
        kmm, kev = emit_projT(p, wb0[:, :, :], xT, KT, NE, bias_col=bk_s,
                              after=[l_wk, l_bk, qev[-1]])
        l_wv = p.dma("gpsimd", wb0[:, :, :], wvd, after=[kmm[-1]])

        last_ev = kev[-1]
        for ob in range(NE):
            def vmm(t, ob=ob):
                for th in range(4):
                    for ec in range(NE):
                        ins = t.matmul(psA[:, th * 512:(th + 1) * 512],
                                       wb0[:, ec, ob * 128:(ob + 1) * 128],
                                       xT[:, ec, th * 512:(th + 1) * 512],
                                       start=(ec == 0), stop=(ec == NE - 1))
                return ins
            m = p.step("tensor", vmm, after=[l_wv, last_ev, trs[-1]])
            sev = p.step("scalar", lambda s, ob=ob: s.activation(
                Pn[:, 0, :], psA[:, :], AF.Identity, bias=bv_s[:, ob:ob + 1]),
                after=[m, l_bv])

            def vtr(t, ob=ob):
                for i in range(NT):
                    ins = t.transpose(psB[:, i * 128:(i + 1) * 128],
                                      Pn[:, 0, i * 128:(i + 1) * 128], idn[:, :])
                return ins
            tstep = p.step("tensor", vtr, after=[sev])
            vev = p.step("vector",
                         lambda v, ob=ob: v.tensor_copy(
                             Vn[:, 0:NT, ob * 128:(ob + 1) * 128],
                             psB[:, :].rearrange("p (c q) -> p c q", q=128)),
                         after=[tstep])
            last_ev = vev

        for buf in (QT, KT):
            for ob in range(NE):
                m1 = p.step("vector",
                            lambda v, buf=buf, ob=ob: v.tensor_scalar_min(
                                Pn[:, 1, :], buf[:, ob, :], 0.0),
                            after=[last_ev])
                e1s = p.step("scalar",
                             lambda s: s.activation(Pn2[:, 0, :], Pn[:, 1, :], AF.Exp),
                             after=[m1])
                f1 = p.step("vector",
                            lambda v, buf=buf, ob=ob: v.scalar_tensor_tensor(
                                buf[:, ob, :], buf[:, ob, :], 0.0, Pn2[:, 0, :],
                                ALU.max, ALU.add),
                            after=[e1s])
                last_ev = f1
        feat_done = last_ev

    OT = p.sbuf("OT", [128, NE, T], BF16)
    xrs = p.sbuf("xrs", [128, E], F32)
    xon = p.sbuf("xon", [128, E], F32)

    def hs(buf, h):
        return lambda c: buf[(h % 2) * 64:(h % 2) * 64 + 64, h // 2,
                             c * 128:(c + 1) * 128]

    last = feat_done
    for h in range(16):
        q_s, k_s = hs(QT, h), hs(KT, h)
        strs, msks, ktes, ivs, ups, sevs, oevs = {}, {}, {}, {}, {}, {}, {}
        for c in range(NT):
            b = c % 2
            dep = [last] if c == 0 else []
            if c >= 2:
                dep.append(msks[c - 2])

            def strf(t, c=c, b=b, q_s=q_s, k_s=k_s):
                return t.matmul(psA[:, b * 128:(b + 1) * 128], k_s(c), q_s(c),
                                start=True, stop=True)
            strs[c] = p.step("tensor", strf, after=dep)

            dep = [strs[c], l_mu]
            if c >= 2:
                dep.append(ivs[c - 2])

            def mskf(v, c=c, b=b):
                return v.tensor_tensor(Pn2[:, 0, b * 128:(b + 1) * 128],
                                       psA[:, b * 128:(b + 1) * 128],
                                       mup01[:, :], ALU.mult)
            msks[c] = p.step("vector", mskf, after=dep)

            dep = [last] if c == 0 else [ktes[c - 2]] if c >= 2 else []

            def ktrf(t, c=c, b=b, k_s=k_s, h=h):
                hb = (h % 2) * 64
                return t.transpose(psB[:, 1792 + b * 64:1792 + (b + 1) * 64],
                                   k_s(c), idn[hb:hb + 64, hb:hb + 64])
            ktr = p.step("tensor", ktrf, after=dep)
            ktes[c] = p.step("vector",
                             lambda v, c=c, b=b: v.tensor_copy(
                                 Knc[:, b, :], psB[:, 1792 + b * 64:1792 + b * 64 + 64]),
                             after=[ktr])

            dep = [msks[c]]
            if c >= 1:
                dep.append(sevs[c - 1])
            if c >= 2:
                dep.append(oevs[c - 2])

            hb = (h % 2) * 64

            def ivf(t, c=c, b=b, h=h, q_s=q_s, hb=hb):
                ins = t.matmul(psC[0:64, b * 128:(b + 1) * 128],
                               Vn[:, c, h * 64:(h + 1) * 64],
                               Pn2[:, 0, b * 128:(b + 1) * 128],
                               start=True, stop=(c == 0), skip_group_check=True)
                if c > 0:
                    ins = t.matmul(psC[0:64, b * 128:(b + 1) * 128],
                                   st_sb[hb:hb + 64, :], q_s(c), start=False, stop=True,
                                   skip_group_check=True)
                return ins
            ivs[c] = p.step("tensor", ivf, after=dep)

            def upf(t, c=c, b=b, h=h, hb=hb):
                return t.matmul(psA[hb:hb + 64, 1920:1984], Knc[:, b, :],
                                Vn[:, c, h * 64:(h + 1) * 64],
                                start=(c == 0), stop=True, skip_group_check=True,
                                tile_position=(0, hb))
            ups[c] = p.step("tensor", upf, after=[ktes[c], ivs[c]])

            if c < NT - 1:
                sevs[c] = p.step("vector",
                                 lambda v, hb=hb: v.tensor_copy(st_sb[hb:hb + 64, :],
                                                                psA[hb:hb + 64, 1920:1984]),
                                 after=[ups[c]] + ([ivs[c]] if True else []))
            oevs[c] = p.step("scalar",
                             lambda s, c=c, b=b, h=h: s.activation(
                                 OT[(h % 2) * 64:(h % 2) * 64 + 64, h // 2,
                                    c * 128:(c + 1) * 128],
                                 psC[0:64, b * 128:(b + 1) * 128], AF.Copy),
                             after=[ivs[c]])
        last = oevs[NT - 1]
        last2 = oevs[NT - 2]

    st_last = emit_out_residual(p, OT, wb1, x_in, x_out, [last, l_wo, l_bo],
                                aug_row=(p.sb["ones_row"][:1, :], bo_s))
    p.step("gpsimd", lambda g: g.engine_nop(), after=[st_last])
    return p


def _l1a_host(inputs):
    d = {}
    d["wqT"] = tile_lhsT(np.asarray(inputs["L1_attn_a_Wq"]).T)
    d["wkT"] = tile_lhsT(np.asarray(inputs["L1_attn_a_Wk"]).T)
    d["wvT"] = tile_lhsT(np.asarray(inputs["L1_attn_a_Wv"]).T)
    d["woT"] = tile_lhsT(np.asarray(inputs["L1_attn_a_Wo"]).T)
    d["bq"] = col128(inputs["L1_attn_a_bq"])
    d["bk"] = col128(inputs["L1_attn_a_bk"])
    d["bv"] = col128(inputs["L1_attn_a_bv"])
    d["bo"] = np.asarray(inputs["L1_attn_a_bo"], dtype=np.float32).reshape(1, E)
    d["c_idn"] = np.eye(128, dtype=np.float32)
    r = np.arange(128)
    d["c_mup01"] = (r[:, None] <= r[None, :]).astype(np.float32)
    return d


def tile_lhsT_blockmajor(W):
    C, P = W.shape
    cc, nb = C // 128, P // 128
    return np.ascontiguousarray(
        W.reshape(cc, 128, nb, 128).transpose(1, 2, 0, 3)).astype(np.float32)


def build_l0_ffn(nl=2, half=False, only_hw=None, no_pv=(), no_pte=(), stage_hw=None, dcut=None):
    p = Prog("l0f")
    nc = p.nc
    x_in = nc.dram_tensor("x", [128, NT * E], F32, kind="ExternalInput").ap()
    x_out = nc.dram_tensor("xo", [128, NT * E], F32, kind="ExternalOutput").ap()
    wqkv_d = [nc.dram_tensor(f"wqkv{l}", [128, NE * 1536], F32, kind="ExternalInput").ap()
              for l in range(2)]
    qs_d = [nc.dram_tensor(f"qs{l}", [1, 64], F32, kind="ExternalInput").ap() for l in range(2)]
    ks_d = [nc.dram_tensor(f"ks{l}", [1, 64], F32, kind="ExternalInput").ap() for l in range(2)]
    wo_d = [nc.dram_tensor(f"wo{l}", [128, 4 * 1024], F32, kind="ExternalInput").ap()
            for l in range(2)]
    lg_d = [nc.dram_tensor(f"lg{l}", [128, NE], F32, kind="ExternalInput").ap() for l in range(2)]
    lb_d = [nc.dram_tensor(f"lb{l}", [128, NE], F32, kind="ExternalInput").ap() for l in range(2)]
    w1_d = [nc.dram_tensor(f"w1{l}", [128, 44, NE, 128], F32, kind="ExternalInput").ap()
            for l in range(2)]
    b1_d = [nc.dram_tensor(f"b1{l}", [128, 44], F32, kind="ExternalInput").ap() for l in range(2)]
    w2_d = [nc.dram_tensor(f"w2{l}", [128, 22 * 1024], F32, kind="ExternalInput").ap()
            for l in range(2)]
    b2_d = [nc.dram_tensor(f"b2{l}", [1, 1024], F32, kind="ExternalInput").ap() for l in range(2)]
    axq_d = nc.dram_tensor("ropeAxq", [128, NT * 64], F32, kind="ExternalInput").ap()
    bxq_d = nc.dram_tensor("ropeBxq", [128, NT * 64], F32, kind="ExternalInput").ap()
    axk_d = nc.dram_tensor("ropeAxk", [128, NT * 64], F32, kind="ExternalInput").ap()
    bxk_d = nc.dram_tensor("ropeBxk", [128, NT * 64], F32, kind="ExternalInput").ap()

    xf_scr = nc.dram_tensor("xf_scr", [128, NT * E], F32, kind="Internal").ap()
    map_scr = nc.dram_tensor("map_scr", [2, 2176], F32, kind="Internal").ap()
    row_scr = nc.dram_tensor("row_scr", [2, 2048], F32, kind="Internal").ap()

    ld = alloc_common(p, need_masks=("mlow", "mup"))
    psA, psB, psC = p.sb["psA"], p.sb["psB"], p.sb["psC"]
    Pn, Pn2, idn, scr, stat, rmax = (p.sb[k] for k in
                                     ("Pn", "Pn2", "idn", "scr", "stat", "rmax"))
    ones_row = p.sb["ones_row"]
    onesf = p.sbuf("onesf", [1, 128], F32)
    idnf = p.sbuf("idnf", [128, 128], F32)
    fstat = p.sbuf("fstat", [128, 48], F32)
    smap = p.sbuf("smap", [128, 4, 16], F32)
    smap2 = p.sbuf("smap2", [16, 4, 128], F32)
    xrs = p.sbuf("xrs", [128, E], F32)
    xon = p.sbuf("xon", [128, E], F32)
    import os
    if os.environ.get("L0F_DBG"):
        p.sbuf("dbg_snap", [64, 512], BF16)
        p.sbuf("dbg_snap2", [128, 512], BF16)
    ofm = p.step("vector", lambda v: v.memset(onesf[:1, :], 1.0))
    c_idn2 = nc.dram_tensor("c_idn2", [128, 128], F32, kind="ExternalInput").ap()
    l_idnf = p.dma("gpsimd", idnf[:, :], c_idn2)

    with ExitStack() as lv0:
        e0 = lv0.enter_context
        xdb = e0(nc.sbuf_tensor("xdb", [128, 2, E], F32))
        xfo = e0(nc.sbuf_tensor("xfo", [128, 2, E], F32))
        st0 = []
        for i in range(NT):
            b = i % 2
            o = b * 12
            xl = p.dma("sync", xdb[:, b, :], x_in[:, i * E:(i + 1) * E],
                       after=[st0[i - 2]] if i >= 2 else [])

            def stats1(v, i=i, o=o, b=b):
                v.scalar_tensor_tensor(scr[:, 0:E], xdb[:, b, :], 1.0, xdb[:, b, :],
                                       ALU.mult, ALU.mult, accum_out=stat[:, o + 1:o + 2])
                return v.tensor_reduce(stat[:, o:o + 1], xdb[:, b, :], AX.X, ALU.add)
            s1 = p.step("vector", stats1, after=[xl])
            s2 = p.step("vector", lambda v, o=o: (
                v.tensor_scalar(stat[:, o + 2:o + 3], stat[:, o:o + 1], 1.0 / E, None, ALU.mult),
                v.tensor_scalar(stat[:, o + 3:o + 4], stat[:, o + 1:o + 2], 1.0 / E,
                                None, ALU.mult))[-1], after=[s1])
            s3 = p.step("vector", lambda v, o=o: (
                v.scalar_tensor_tensor(stat[:, o + 4:o + 5], stat[:, o + 2:o + 3], 1.0,
                                       stat[:, o + 2:o + 3], ALU.mult, ALU.mult),
                v.tensor_scalar(stat[:, o + 5:o + 6], stat[:, o + 3:o + 4], 1e-5,
                                None, ALU.add))[-1], after=[s2])
            s4 = p.step("vector", lambda v, o=o: v.tensor_tensor(
                stat[:, o + 6:o + 7], stat[:, o + 5:o + 6], stat[:, o + 4:o + 5],
                ALU.subtract), after=[s3])
            a1 = p.step("scalar", lambda s, o=o: s.activation(
                stat[:, o + 7:o + 8], stat[:, o + 6:o + 7], AF.Ln), after=[s4])
            a2 = p.step("scalar", lambda s, o=o: s.activation(
                stat[:, o + 8:o + 9], stat[:, o + 7:o + 8], AF.Exp, scale=-0.5), after=[a1])
            a3 = p.step("scalar", lambda s, o=o: s.mul(
                stat[:, o + 9:o + 10], stat[:, o + 2:o + 3], stat[:, o + 8:o + 9]), after=[a2])
            a4 = p.step("scalar", lambda s, o=o: s.mul(
                stat[:, o + 10:o + 11], stat[:, o + 9:o + 10], -1.0), after=[a3])
            ap = p.step("scalar", lambda s, i=i, o=o, b=b: s.activation(
                xfo[:, b, :], xdb[:, b, :], AF.Identity,
                bias=stat[:, o + 10:o + 11], scale=stat[:, o + 8:o + 9]),
                after=[a4] + ([st0[i - 2]] if i >= 2 else []))
            st0.append(p.dma("sync", xf_scr[:, i * E:(i + 1) * E], xfo[:, b, :],
                             after=[ap]))
        prev = st0[-1]

    H8, D64, W512 = 8, 64, 512

    def _layer(l, prev):
        with ExitStack() as lyr:
            el = lyr.enter_context
            sxFT = el(nc.sbuf_tensor(f"sxFT{l}", [128, NE, 2050], BF16))
            with ExitStack() as lAD:
                ea = lAD.enter_context
                wqkv = ea(nc.sbuf_tensor(f"sb_wqkv{l}", [128, NE, 1536], BF16))
                woi = ea(nc.sbuf_tensor(f"sb_woi{l}", [128, 4, 1024], BF16))
                qkvn = ea(nc.sbuf_tensor(f"sb_qkvn{l}", [128, 2, 1024], F32))
                qwk = ea(nc.sbuf_tensor(f"sb_qwk{l}", [128, 2, 512], BF16))
                kwk = ea(nc.sbuf_tensor(f"sb_kwk{l}", [128, 2, 512], BF16))
                qwf = ea(nc.sbuf_tensor(f"sb_qwf{l}", [128, 1, 512], F32))
                kwf = ea(nc.sbuf_tensor(f"sb_kwf{l}", [128, 1, 512], F32))
                qT = ea(nc.sbuf_tensor(f"sb_qT{l}", [128, 4, T], BF16))
                kT = ea(nc.sbuf_tensor(f"sb_kT{l}", [128, 4, T], BF16))
                vn = ea(nc.sbuf_tensor(f"sb_vn{l}", [128, NT, 512], BF16))
                PTl = ea(nc.sbuf_tensor(f"sb_PTl{l}", [128, 8, 512], BF16))
                OTl = ea(nc.sbuf_tensor(f"sb_OTl{l}", [128, 4, T], BF16))
                Axq = ea(nc.sbuf_tensor(f"sb_Axq{l}", [128, NT, 64], BF16))
                Bxq = ea(nc.sbuf_tensor(f"sb_Bxq{l}", [128, NT, 64], BF16))
                Axk = ea(nc.sbuf_tensor(f"sb_Axk{l}", [128, NT, 64], BF16))
                Bxk = ea(nc.sbuf_tensor(f"sb_Bxk{l}", [128, NT, 64], BF16))
                qsc = ea(nc.sbuf_tensor(f"sb_qsc{l}", [128, 64], BF16))
                ksc = ea(nc.sbuf_tensor(f"sb_ksc{l}", [128, 64], BF16))
                xdb1 = ea(nc.sbuf_tensor(f"sb_xdb1{l}", [128, 2, E], F32))
                xlb = ea(nc.sbuf_tensor(f"sb_xlb{l}", [128, 2, E], BF16))
                p.sb[f"dbg_qT{l}"] = qT
                p.sb[f"dbg_kT{l}"] = kT
                p.sb[f"dbg_vn{l}"] = vn
                p.sb[f"dbg_OTl{l}"] = OTl
                p.sb[f"dbg_sxFT{l}"] = sxFT
                p.sb[f"dbg_PTl{l}"] = PTl

                l_wqkv = p.dma("gpsimd", wqkv[:, :, :], wqkv_d[l], after=[prev])
                l_wo = p.dma("gpsimd", woi[:, :, :], wo_d[l], after=[prev])
                l_tab = [p.dma("gpsimd", Axq[:, :, :], axq_d, after=[prev]),
                         p.dma("gpsimd", Bxq[:, :, :], bxq_d, after=[prev]),
                         p.dma("gpsimd", Axk[:, :, :], axk_d, after=[prev]),
                         p.dma("gpsimd", Bxk[:, :, :], bxk_d, after=[prev])]
                l_qs = p.dma("gpsimd", scr[0:1, 0:64].bitcast(F32), qs_d[l], after=[prev, ofm])
                qbc = p.step("tensor", lambda t: t.matmul(psC[:, 0:64], onesf[:1, :],
                                                          scr[0:1, 0:64], start=True,
                                                          stop=True), after=[l_qs])
                qbe = p.step("vector", lambda v: v.tensor_copy(qsc[:, :], psC[:, 0:64]),
                             after=[qbc])
                l_ks = p.dma("gpsimd", scr[0:1, 64:128].bitcast(F32), ks_d[l], after=[qbc])
                kbc = p.step("tensor", lambda t: t.matmul(psC[:, 64:128], onesf[:1, :],
                                                          scr[0:1, 64:128], start=True,
                                                          stop=True), after=[l_ks, qbe])
                kbe = p.step("vector", lambda v: v.tensor_copy(ksc[:, :], psC[:, 64:128]),
                             after=[kbc])

                zc = p.step("vector", lambda v: v.memset(sxFT[:, :, 0:1], 0.0),
                            after=[prev])
                trs = []
                tr_s = []
                casts = []
                xls = []
                for i in range(NT):
                    b = i % 2
                    xls.append(p.dma("sync", xdb1[:, b, :], xf_scr[:, i * E:(i + 1) * E],
                                     after=[prev] + ([casts[i - 2]] if i >= 2 else [])))
                    casts.append(p.step("vector",
                                        lambda v, i=i, b=b: v.tensor_copy(xlb[:, b, :],
                                                                          xdb1[:, b, :]),
                                        after=[xls[i]] + ([tr_s[i - 2]] if i >= 2 else [])))

                    def trf(t, i=i, b=b):
                        half = b * NE * 128
                        for ch in range(NE):
                            ins = t.transpose(psB[:, half + ch * 128: half + (ch + 1) * 128],
                                              xlb[:, b, ch * 128:(ch + 1) * 128], idn[:, :])
                        return ins
                    tr_s.append(p.step("tensor", trf,
                                       after=[casts[i]] + ([trs[i - 2]] if i >= 2 else [])))

                    def ev(v, i=i, b=b):
                        half = b * NE * 128
                        v.tensor_copy(sxFT[:, 0:4, i * 128:(i + 1) * 128],
                                      psB[:, half:half + 4 * 128]
                                      .rearrange("p (c q) -> p c q", q=128))
                        return v.tensor_copy(sxFT[:, 4:NE, 1 + i * 128:1 + (i + 1) * 128],
                                             psB[:, half + 4 * 128:half + NE * 128]
                                             .rearrange("p (c q) -> p c q", q=128))
                    trs.append(p.step("vector", ev, after=[tr_s[i], zc]))

                def sx_lhs(ec, i):
                    return sxFT[:, ec, i * 128: i * 128 + 128]

                vcs = []
                qdone = []
                tqs = []
                for i in range(NT):
                    b = i % 2

                    def qmm(t, i=i):
                        for fc in range(3):
                            for ec in range(NE):
                                ins = t.matmul(psA[:, fc * 512:(fc + 1) * 512],
                                               sx_lhs(ec, i),
                                               wqkv[:, ec, fc * 512:(fc + 1) * 512],
                                               start=(ec == 0), stop=(ec == NE - 1))
                        return ins
                    dep = [l_wqkv, trs[-1]]
                    if i >= 1:
                        dep.append(qdone[-1])
                        dep.append(vcs[-1])
                    m = p.step("tensor", qmm, after=dep)
                    qe = p.step("scalar", lambda s, i=i, b=b: s.activation(
                        qkvn[:, b, :], psA[:, 0:1024], AF.Copy),
                        after=[m] + ([qdone[i - 2]] if i >= 2 else []))
                    vcs.append(p.step("vector",
                                      lambda v, i=i: v.tensor_copy(
                                          vn[:, i, :], psA[:, 1024:1536]),
                                      after=[m] + ([qdone[i - 2]] if i >= 2
                                                   else [])))
                    sq1 = p.step("vector", lambda v, b=b: v.tensor_tensor(
                        scr[:, 0:1024], qkvn[:, b, 0:1024], qkvn[:, b, 0:1024], ALU.mult),
                        after=[qe])
                    rd1 = p.step("vector", lambda v: v.tensor_reduce(
                        fstat[:, 0:16], scr[:, 0:1024].rearrange("p (h d) -> p h d", d=64),
                        AX.X, ALU.add), after=[sq1])
                    ep1 = p.step("vector", lambda v: v.tensor_scalar(
                        fstat[:, 16:32], fstat[:, 0:16], 1.0, 1e-24, ALU.mult, ALU.add),
                        after=[rd1])
                    ln1 = p.step("scalar", lambda s: s.activation(
                        fstat[:, 16:32], fstat[:, 16:32], AF.Ln), after=[ep1])
                    ex1 = p.step("scalar", lambda s: s.activation(
                        fstat[:, 16:32], fstat[:, 16:32], AF.Exp, scale=-0.5), after=[ln1])
                    qs1 = p.step("vector", lambda v, b=b: v.tensor_tensor(
                        qwf[:, 0, :].rearrange("p (h d) -> p h d", d=64),
                        qkvn[:, b, 0:512].rearrange("p (h d) -> p h d", d=64),
                        fstat[:, 16:24].unsqueeze(2).to_broadcast([128, 8, 64]),
                        ALU.mult), after=[ex1] + ([tqs[i - 2]] if i >= 2 else []))
                    qs2 = p.step("vector", lambda v, b=b: v.tensor_tensor(
                        qwf[:, 0, :].rearrange("p (h d) -> p h d", d=64),
                        qwf[:, 0, :].rearrange("p (h d) -> p h d", d=64),
                        qsc[:, :].unsqueeze(1).to_broadcast([128, 8, 64]),
                        ALU.mult), after=[qs1, qbe])
                    ks1 = p.step("vector", lambda v, b=b: v.tensor_tensor(
                        kwf[:, 0, :].rearrange("p (h d) -> p h d", d=64),
                        qkvn[:, b, 512:1024].rearrange("p (h d) -> p h d", d=64),
                        fstat[:, 24:32].unsqueeze(2).to_broadcast([128, 8, 64]),
                        ALU.mult), after=[ex1] + ([tqs[i - 2]] if i >= 2 else []))
                    ks2 = p.step("vector", lambda v, b=b: v.tensor_tensor(
                        kwf[:, 0, :].rearrange("p (h d) -> p h d", d=64),
                        kwf[:, 0, :].rearrange("p (h d) -> p h d", d=64),
                        ksc[:, :].unsqueeze(1).to_broadcast([128, 8, 64]),
                        ALU.mult), after=[ks1, kbe])
                    rq1 = p.step("vector", lambda v, i=i, b=b: v.tensor_tensor(
                        scr[:, 0:512].rearrange("p (h d) -> p h d", d=64),
                        qwf[:, 0, :].rearrange("p (h d) -> p h d", d=64),
                        Axq[:, i, :].unsqueeze(1).to_broadcast([128, 8, 64]),
                        ALU.mult), after=[qs2] + l_tab)
                    rq2 = p.step("vector", lambda v, i=i, b=b: v.tensor_tensor(
                        scr[:, 512:1024].rearrange("p (h two hf) -> p h two hf", two=2, hf=32),
                        qwf[:, 0, :].rearrange("p (h two hf) -> p h two hf", two=2, hf=32)
                        [:, :, ::-1, :],
                        Bxq[:, i, :].rearrange("p (two hf) -> p two hf", two=2)
                        .unsqueeze(1).to_broadcast([128, 8, 2, 32]),
                        ALU.mult), after=[qs2])
                    rq3 = p.step("vector", lambda v, b=b: v.tensor_tensor(
                        qwk[:, b, :], scr[:, 0:512], scr[:, 512:1024], ALU.add),
                        after=[rq1, rq2] + ([tqs[i - 2]] if i >= 2 else []))
                    rk1 = p.step("vector", lambda v, i=i, b=b: v.tensor_tensor(
                        scr[:, 0:512].rearrange("p (h d) -> p h d", d=64),
                        kwf[:, 0, :].rearrange("p (h d) -> p h d", d=64),
                        Axk[:, i, :].unsqueeze(1).to_broadcast([128, 8, 64]),
                        ALU.mult), after=[ks2, rq3])
                    rk2 = p.step("vector", lambda v, i=i, b=b: v.tensor_tensor(
                        scr[:, 512:1024].rearrange("p (h two hf) -> p h two hf", two=2, hf=32),
                        kwf[:, 0, :].rearrange("p (h two hf) -> p h two hf", two=2, hf=32)
                        [:, :, ::-1, :],
                        Bxk[:, i, :].rearrange("p (two hf) -> p two hf", two=2)
                        .unsqueeze(1).to_broadcast([128, 8, 2, 32]),
                        ALU.mult), after=[ks2, rq3])
                    rk3 = p.step("vector", lambda v, b=b: v.tensor_tensor(
                        kwk[:, b, :], scr[:, 0:512], scr[:, 512:1024], ALU.add),
                        after=[rk1, rk2] + ([tqs[i - 2]] if i >= 2 else []))
                    def qtr(t, i=i, b=b):
                        for ch in range(4):
                            ins = t.transpose(psB[:, b * 1024 + ch * 128:
                                                  b * 1024 + (ch + 1) * 128],
                                              qwk[:, b, ch * 128:(ch + 1) * 128], idn[:, :])
                        for ch in range(4):
                            ins = t.transpose(psB[:, b * 1024 + 512 + ch * 128:
                                                  b * 1024 + 512 + (ch + 1) * 128],
                                              kwk[:, b, ch * 128:(ch + 1) * 128], idn[:, :])
                        return ins
                    tq = p.step("tensor", qtr, after=[rq3, rk3] +
                                ([qdone[i - 2]] if i >= 2 else []))
                    tqs.append(tq)

                    def qte(v, i=i, b=b):
                        v.tensor_copy(qT[:, 0:4, i * 128:(i + 1) * 128],
                                      psB[:, b * 1024:b * 1024 + 512]
                                      .rearrange("p (c q) -> p c q", q=128))
                        return v.tensor_copy(kT[:, 0:4, i * 128:(i + 1) * 128],
                                             psB[:, b * 1024 + 512:b * 1024 + 1024]
                                             .rearrange("p (c q) -> p c q", q=128))
                    qdone.append(p.step("vector", qte, after=[tq]))

                last = qdone[-1]
                for h in range(H8):
                    hb = (h % 2) * 64
                    hc = h // 2
                    for w in range(4):
                        if only_hw is not None and (h, w) not in only_hw:
                            continue
                        smask = []
                        if w == 0:
                            nstrip = 4
                            kg0 = 0
                        else:
                            nstrip = 8
                            kg0 = (w - 1) * 4
                        sc_s, rm_s, ex_s, dg_s, trp, pte = {}, {}, {}, {}, {}, {}
                        chain = last
                        for qtl in range(4):
                            if w == 0:
                                klo, khi = 0, (qtl + 1) * 128
                                mpos = [(qtl, "mlow")]
                            else:
                                klo, khi = qtl * 128, 512 + (qtl + 1) * 128
                                mpos = [(qtl, "mup"), (4 + qtl, "mlow")]
                            gq = w * 4 + qtl
                            nvis = khi - klo

                            def scf(t, klo=klo, nvis=nvis, gq=gq, hb=hb, hc=hc, kg0=kg0):
                                k0 = 0
                                while k0 < nvis:
                                    k1 = min(k0 + 512, nvis)
                                    ins = t.matmul(psA[:, k0:k1],
                                                   qT[hb:hb + 64, hc, gq * 128:(gq + 1) * 128],
                                                   kT[hb:hb + 64, hc,
                                                      kg0 * 128 + klo + k0:
                                                      kg0 * 128 + klo + k1],
                                                   start=True, stop=True)
                                    k0 = k1
                                return ins
                            sc0 = p.step("tensor", scf, after=[chain])

                            def mskadd(v, mpos=mpos, klo=klo):
                                for (blk, mn) in mpos:
                                    rb = blk - klo // 128
                                    ins = v.tensor_tensor(
                                        psA[:, rb * 128:(rb + 1) * 128],
                                        psA[:, rb * 128:(rb + 1) * 128],
                                        p.sb[mn][:, :], ALU.add)
                                return ins
                            sc_s[qtl] = p.step("vector", mskadd, after=[sc0])
                            stg = None if stage_hw is None else stage_hw.get((h, w))
                            if stg == 1:
                                chain = sc_s[qtl]
                                continue

                            o = (qtl % 2) * 8
                            rm0 = p.step("vector", lambda v, o=o, nvis=nvis:
                                         v.tensor_reduce(rmax[:, o:o + 1],
                                                         psA[:, 0:nvis], AX.X, ALU.max),
                                         after=[sc_s[qtl]])
                            rm_s[qtl] = p.step("vector", lambda v, o=o: v.tensor_scalar(
                                rmax[:, o + 1:o + 2], rmax[:, o:o + 1], -1.0, None,
                                ALU.mult), after=[rm0])
                            ex_s[qtl] = p.step("scalar", lambda s, o=o, qtl=qtl,
                                               nvis=nvis: s.activation(
                                Pn[:, qtl % 2, 0:nvis], psA[:, 0:nvis], AF.Exp,
                                scale=1.0, bias=rmax[:, o + 1:o + 2],
                                accum_out=rmax[:, o + 3:o + 4]), after=[rm_s[qtl]])
                            dgr = p.step("vector", lambda v, o=o: v.reciprocal(
                                rmax[:, o + 2:o + 3], rmax[:, o + 3:o + 4]),
                                after=[ex_s[qtl]])
                            dg_s[qtl] = p.step("vector", lambda v, o=o, qtl=qtl,
                                               nvis=nvis: v.tensor_scalar(
                                Pn2[:, 0, 0:nvis], Pn[:, qtl % 2, 0:nvis],
                                rmax[:, o + 2:o + 3], None, ALU.mult), after=[dgr])

                            def trpf(t, nvis=nvis):
                                for kc in range(nvis // 128):
                                    ins = t.transpose(psB[:, kc * 128:(kc + 1) * 128],
                                                      Pn2[:, 0, kc * 128:(kc + 1) * 128],
                                                      idn[:, :])
                                return ins
                            trp[qtl] = p.step("tensor", trpf, after=[dg_s[qtl]])
                            pte[qtl] = p.step("vector", lambda v, qtl=qtl, nvis=nvis:
                                              v.tensor_copy(
                                PTl[:, 0:nvis // 128, qtl * 128:(qtl + 1) * 128],
                                psB[:, 0:nvis].rearrange("p (c q) -> p c q", q=128)),
                                after=[trp[qtl]])
                            chain = pte[qtl]
                        def pvf(t, w=w, kg0=kg0, h=h):
                            for qtl in range(4):
                                klo_t = 0 if w == 0 else qtl
                                nkc = (qtl + 1) if w == 0 else 5
                                for j in range(nkc):
                                    gk = kg0 + klo_t + j
                                    ins = t.matmul(psA[0:64, 1024 + qtl * 128:
                                                       1024 + (qtl + 1) * 128],
                                                   vn[:, gk, h * 64:(h + 1) * 64],
                                                   PTl[:, j, qtl * 128:(qtl + 1) * 128],
                                                   start=(j == 0), stop=(j == nkc - 1),
                                                   skip_group_check=True)
                            return ins
                        if (h, w) in no_pv:
                            last = pte.get(3, chain)
                            chain = last
                            continue
                        pv = p.step("tensor", pvf,
                                    after=[pte[q] for q in range(4)] + [vcs[-1], last])
                        oev = p.step("scalar", lambda s, h=h, w=w, hb=hb, hc=hc:
                                     s.activation(OTl[hb:hb + 64, hc,
                                                      w * 512:(w + 1) * 512],
                                                  psA[0:64, 1024:1536], AF.Copy), after=[pv])
                        chain = oev
                        if (h, w) == (0, 0) and "dbg_snap" in p.sb:
                            oev = p.step("vector", lambda v: v.tensor_copy(
                                p.sb["dbg_snap"][0:64, :], psA[0:64, 1024:1536]), after=[oev])
                            oev = p.step("vector", lambda v: v.tensor_copy(
                                p.sb["dbg_snap2"][:, 0:128], PTl[:, 0, 0:128]), after=[oev])
                            oev = p.step("vector", lambda v: v.tensor_copy(
                                p.sb["dbg_snap2"][:, 128:256], PTl[:, 1, 128:256]), after=[oev])
                        last = oev

                if half == "attn":
                    return last
                wo_res = []
                adds = []
                xrl2 = []
                dcsts = []
                devs = []
                for i in range(NT):
                    xrl2.append(p.dma("sync", xrs[:, :], xf_scr[:, i * E:(i + 1) * E],
                                      after=[adds[i - 1]] if i >= 1 else [last]))

                    def wmm(t, i=i):
                        for fc in range(2):
                            for ec in range(4):
                                ins = t.matmul(psC[:, fc * 512:(fc + 1) * 512],
                                               OTl[:, ec, i * 128:(i + 1) * 128],
                                               woi[:, ec, fc * 512:(fc + 1) * 512],
                                               start=(ec == 0), stop=(ec == 3))
                        return ins
                    dep = [last, l_wo]
                    if i >= 1:
                        dep.append(adds[i - 1])
                    m = p.step("tensor", wmm, after=dep)
                    a = p.step("vector", lambda v, i=i: v.tensor_tensor(
                        xon[:, :], psC[:, 0:E], xrs[:, :], ALU.add),
                        after=[m, xrl2[i]] + ([wo_res[i - 1]] if i >= 1 else [])
                        + ([dcsts[i - 1]] if i >= 1 else []))
                    adds.append(a)
                    s1 = p.step("vector", lambda v, i=i: (
                        v.tensor_reduce(smap[:, 0, i:i + 1], xon[:, 0:512], AX.X, ALU.add),
                        v.tensor_reduce(smap[:, 1, i:i + 1], xon[:, 512:1024], AX.X, ALU.add))[-1],
                        after=[a])
                    s2 = p.step("vector", lambda v, i=i: (
                        v.scalar_tensor_tensor(scr[:, 0:512], xon[:, 0:512], 1.0,
                                               xon[:, 0:512], ALU.mult, ALU.mult,
                                               accum_out=smap[:, 2, i:i + 1]),
                        v.scalar_tensor_tensor(scr[:, 512:1024], xon[:, 512:1024], 1.0,
                                               xon[:, 512:1024], ALU.mult, ALU.mult,
                                               accum_out=smap[:, 3, i:i + 1]))[-1],
                        after=[s1])
                    wo_res.append(p.dma("sync", xf_scr[:, i * E:(i + 1) * E], xon[:, :],
                                        after=[s2]))
                    dc = p.step("vector", lambda v, i=i: v.tensor_copy(
                        xlb[:, i % 2, :], xon[:, :]), after=[a])
                    dcsts.append(dc)

                    def dtrf(t, i=i):
                        half_ = (i % 2) * NE * 128
                        for ch in range(NE):
                            ins = t.transpose(
                                psB[:, half_ + ch * 128: half_ + (ch + 1) * 128],
                                xlb[:, i % 2, ch * 128:(ch + 1) * 128], idn[:, :])
                        return ins
                    dtr = p.step("tensor", dtrf, after=[dc] +
                                 ([devs[i - 2]] if i >= 2 else []))

                    def devf(v, i=i):
                        half_ = (i % 2) * NE * 128
                        v.tensor_copy(sxFT[:, 0:4, i * 128:(i + 1) * 128],
                                      psB[:, half_:half_ + 4 * 128]
                                      .rearrange("p (c q) -> p c q", q=128))
                        return v.tensor_copy(
                            sxFT[:, 4:NE, 1 + i * 128:1 + (i + 1) * 128],
                            psB[:, half_ + 4 * 128:half_ + NE * 128]
                            .rearrange("p (c q) -> p c q", q=128))
                    devs.append(p.step("vector", devf, after=[dtr]))
                prev = wo_res[-1]
                stat_done = [wo_res[-1], s2]
                if half and dcut == "wo":
                    return prev

                smt = []
                nmi = 4
                if isinstance(dcut, str) and dcut.startswith("smt") and len(dcut) > 3:
                    nmi = int(dcut[3:])
                for mi in range(nmi):
                    trm = p.step("tensor", lambda t, mi=mi: t.transpose(
                        psA[0:16, 1536 + mi * 128:1536 + (mi + 1) * 128],
                        smap[:, mi, :], idnf[:, :]), after=stat_done +
                        ([smt[-1]] if smt else []))
                    evm = p.step("vector", lambda v, mi=mi: v.tensor_copy(
                        smap2[:, mi, :], psA[0:16, 1536 + mi * 128:1536 + (mi + 1) * 128]),
                        after=[trm])
                    smt.append(evm)
                if half and isinstance(dcut, str) and dcut.startswith("smt"):
                    return smt[-1]
                sm_z0 = p.step("vector", lambda v: v.memset(scr[0:1, 0:2], 0.0),
                               after=[smt[1], smt[3]])
                sm_zd0 = p.dma("sync", map_scr[0, 0:1].unsqueeze(0),
                               scr[0:1, 0:1], after=[sm_z0])
                sm_zd = p.dma("sync", map_scr[1, 0:1].unsqueeze(0),
                              scr[0:1, 1:2], after=[sm_z0, sm_zd0])
                sm_st1 = p.dma("sync",
                               map_scr[0, 1:2049].rearrange("(i p) -> i p", p=128),
                               smap2[:, 1, :], after=[smt[1]])
                sm_st2 = p.dma("sync",
                               map_scr[1, 1:2049].rearrange("(i p) -> i p", p=128),
                               smap2[:, 3, :], after=[smt[3]])
                sm_l1 = p.dma("sync", smap2[:, 1, :],
                              map_scr[0, 0:2048].rearrange("(i p) -> i p", p=128),
                              after=[sm_st1, sm_zd])
                sm_l1b = sm_l1
                sm_l2 = p.dma("sync", smap2[:, 3, :],
                              map_scr[1, 0:2048].rearrange("(i p) -> i p", p=128),
                              after=[sm_st2, sm_zd])
                sm_l2b = sm_l2
                if half and dcut == "maps":
                    return sm_l2
                cm1 = p.step("vector", lambda v: (
                    v.tensor_tensor(smap2[:, 0, :], smap2[:, 0, :],
                                    smap2[:, 1, :], ALU.add),
                    v.tensor_tensor(smap2[:, 2, :], smap2[:, 2, :],
                                    smap2[:, 3, :], ALU.add))[-1],
                    after=[sm_l1, sm_l1b, sm_l2, sm_l2b, smt[0], smt[2]])
                cm2 = p.step("vector", lambda v: (
                    v.tensor_scalar(smap2[:, 0, :], smap2[:, 0, :], 1.0 / E, None, ALU.mult),
                    v.tensor_scalar(smap2[:, 2, :], smap2[:, 2, :], 1.0 / E, None, ALU.mult))[-1],
                    after=[cm1])
                cm3 = p.step("vector", lambda v: v.scalar_tensor_tensor(
                    smap2[:, 1, :], smap2[:, 0, :], 1.0, smap2[:, 0, :],
                    ALU.mult, ALU.mult), after=[cm2])
                cm4a = p.step("vector", lambda v: v.tensor_tensor(
                    smap2[:, 2, :], smap2[:, 2, :], smap2[:, 1, :], ALU.subtract),
                    after=[cm3])
                cm4 = p.step("vector", lambda v: v.tensor_scalar(
                    smap2[:, 2, :], smap2[:, 2, :], 1.0, 1e-5, ALU.mult, ALU.add),
                    after=[cm4a])
                cm5 = p.step("scalar", lambda s: s.activation(smap2[:, 2, :], smap2[:, 2, :],
                                                              AF.Ln), after=[cm4])
                cm6 = p.step("scalar", lambda s: s.activation(smap2[:, 2, :], smap2[:, 2, :],
                                                              AF.Exp, scale=-0.5), after=[cm5])
                cm7 = p.step("vector", lambda v: v.scalar_tensor_tensor(
                    smap2[:, 1, :], smap2[:, 0, :], -1.0, smap2[:, 2, :],
                    ALU.mult, ALU.mult), after=[cm6])
                rw1 = p.dma("sync", row_scr[0, :].rearrange("(i p) -> i p", p=128),
                            smap2[:, 2, :], after=[cm7])
                rw2 = p.dma("sync", row_scr[1, :].rearrange("(i p) -> i p", p=128),
                            smap2[:, 1, :], after=[cm7])
                prev = adds[-1]
            if half:
                return prev
            with ExitStack() as lEF:
                ef = lEF.enter_context
                xln2T = ef(nc.sbuf_tensor(f"sb_xln2T{l}", [128, NE, T], BF16))
                hT = ef(nc.sbuf_tensor(f"sb_hT{l}", [128, 22, 1024], BF16))
                w2i = ef(nc.sbuf_tensor(f"sb_w2i{l}", [128, 22, 1024], BF16))
                w1r = ef(nc.sbuf_tensor(f"sb_w1r{l}", [128, 4, NE, 128], BF16))
                lg_s = ef(nc.sbuf_tensor(f"sb_lg_s{l}", [128, NE], F32))
                lb_s = ef(nc.sbuf_tensor(f"sb_lb_s{l}", [128, NE], F32))
                b1c = ef(nc.sbuf_tensor(f"sb_b1c{l}", [128, 44], F32))
                b2r = ef(nc.sbuf_tensor(f"sb_b2r{l}", [1, 1024], BF16))
                rrow = ef(nc.sbuf_tensor(f"sb_rrow{l}", [1, 2, 2048], F32))

                l_lg = p.dma("gpsimd", lg_s[:, :], lg_d[l], after=[rw2])
                l_lb = p.dma("gpsimd", lb_s[:, :], lb_d[l], after=[rw2])
                l_b1 = p.dma("gpsimd", b1c[:, :], b1_d[l], after=[rw2])
                l_b2 = p.dma("gpsimd", b2r[:1, :], b2_d[l], after=[rw2])
                l_w2 = p.dma("gpsimd", w2i[:, :, :], w2_d[l], after=[rw2])
                lrow1 = p.dma("sync", rrow[0:1, 0, :], row_scr[0, :].unsqueeze(0),
                              after=[rw1])
                lrow2 = p.dma("sync", rrow[0:1, 1, :], row_scr[1, :].unsqueeze(0),
                              after=[rw2])

                def bc_mm(t, row, target_off):
                    for fc in range(4):
                        ins = t.matmul(psA[:, target_off + fc * 512:
                                           target_off + (fc + 1) * 512],
                                       onesf[:1, :], rrow[0:1, row, fc * 512:(fc + 1) * 512],
                                       start=True, stop=True, skip_group_check=True)
                    return ins
                bc1 = p.step("tensor", lambda t: bc_mm(t, 0, 0),
                             after=[lrow1, ofm, prev, devs[-1]])
                ec_steps = []
                for ec in range(NE):
                    def t1f(v, ec=ec):
                        return v.tensor_tensor(xln2T[:, ec, :],
                                               sxFT[:, ec, 0:2048],
                                               psA[:, 0:2048], ALU.mult)
                    ec_steps.append(p.step("vector", t1f, after=[bc1]))
                bc2 = p.step("tensor", lambda t: bc_mm(t, 1, 0),
                             after=[lrow2] + ec_steps)
                ec2 = []
                for ec in range(NE):
                    def t2f(v, ec=ec):
                        return v.tensor_tensor(xln2T[:, ec, :], xln2T[:, ec, :],
                                               psA[:, 0:2048], ALU.add)
                    s_ = p.step("vector", t2f, after=[bc2] + ([ec2[-1]] if ec2 else []))
                    ec2.append(s_)

                    def t3f(v, ec=ec):
                        return v.scalar_tensor_tensor(
                            xln2T[:, ec, :], xln2T[:, ec, :], lg_s[:, ec:ec + 1],
                            lb_s[:, ec:ec + 1].to_broadcast([128, 2048]), ALU.mult, ALU.add)
                    s2_ = p.step("vector", t3f, after=[s_, l_lg, l_lb])
                    ec2.append(s2_)
                xln_done = ec2[-1]

                prev_w = xln_done
                for th in range(2):
                    hsteps = []
                    w1l = {}
                    for j in range(22):
                        sa = j % 2
                        w1l[(j, 0)] = p.dma("gpsimd", w1r[:, sa * 2, :, :],
                                            w1_d[l][:, 2 * j, :, :],
                                            after=[prev_w] if j < 2 else [hsteps[-6]])
                        w1l[(j, 1)] = p.dma("gpsimd", w1r[:, sa * 2 + 1, :, :],
                                            w1_d[l][:, 2 * j + 1, :, :],
                                            after=[prev_w] if j < 2 else [hsteps[-6]])

                        def amm(t, j=j, sa=sa, th=th):
                            for fc in range(2):
                                for ec in range(NE):
                                    ins = t.matmul(psA[:, fc * 512:(fc + 1) * 512],
                                                   w1r[:, sa * 2, ec, :],
                                                   xln2T[:, ec, th * 1024 + fc * 512:
                                                         th * 1024 + (fc + 1) * 512],
                                                   start=(ec == 0), stop=(ec == NE - 1))
                                for ec in range(NE):
                                    ins = t.matmul(psA[:, 1024 + fc * 512:
                                                       1024 + (fc + 1) * 512],
                                                   w1r[:, sa * 2 + 1, ec, :],
                                                   xln2T[:, ec, th * 1024 + fc * 512:
                                                         th * 1024 + (fc + 1) * 512],
                                                   start=(ec == 0), stop=(ec == NE - 1))
                            return ins
                        dep = [w1l[(j, 0)], w1l[(j, 1)], xln_done]
                        if hsteps:
                            dep.append(hsteps[-1])
                        m = p.step("tensor", amm, after=dep)
                        eg_ = p.step("scalar", lambda s, j=j: s.activation(
                            Pn[:, 1, 0:1024], psA[:, 1024:2048], AF.Gelu,
                            bias=b1c[:, 2 * j + 1:2 * j + 2]), after=[m, l_b1])
                        hm = p.step("vector", lambda v, j=j: v.scalar_tensor_tensor(
                            hT[:, j, :], psA[:, 0:1024], b1c[:, 2 * j:2 * j + 1],
                            Pn[:, 1, 0:1024], ALU.add, ALU.mult), after=[eg_])
                        hsteps.extend([m, eg_, hm])
                    wres = []
                    for ii in range(8):
                        i = th * 8 + ii
                        xr2 = p.dma("sync", xrs[:, :], xf_scr[:, i * E:(i + 1) * E],
                                    after=[wres[-1]] if wres else [hsteps[-1]])

                        def w2mm(t, i=i, ii=ii):
                            for fc in range(2):
                                for hc2 in range(22):
                                    ins = t.matmul(psC[:, fc * 512:(fc + 1) * 512],
                                                   hT[:, hc2, ii * 128:(ii + 1) * 128],
                                                   w2i[:, hc2, fc * 512:(fc + 1) * 512],
                                                   start=(hc2 == 0), stop=False)
                                ins = t.matmul(psC[:, fc * 512:(fc + 1) * 512],
                                               ones_row[:1, :],
                                               b2r[:1, fc * 512:(fc + 1) * 512],
                                               start=False, stop=True)
                            return ins
                        dep = [hsteps[-1], l_w2, l_b2]
                        if wres:
                            dep.append(wres[-1])
                        m2 = p.step("tensor", w2mm, after=dep)
                        a2_ = p.step("vector", lambda v, i=i: v.tensor_tensor(
                            xon[:, :], psC[:, 0:E], xrs[:, :], ALU.add),
                            after=[m2, xr2])
                        wres.append(p.dma("sync", xf_scr[:, i * E:(i + 1) * E],
                                          xon[:, :], after=[a2_]))
                    prev_w = wres[-1]
                prev = prev_w

        return prev

    for l in range(nl):
        prev = _layer(l, prev)

    fsteps = []
    for i in range(NT):
        b = i % 2
        lx = p.dma("sync", xrs[:, :], x_in[:, i * E:(i + 1) * E],
                   after=[fsteps[-1]] if fsteps else [prev])
        lf = p.dma("sync", xon[:, :], xf_scr[:, i * E:(i + 1) * E], after=[lx])
        ad = p.step("vector", lambda v: v.tensor_tensor(xon[:, :], xon[:, :],
                                                        xrs[:, :], ALU.add), after=[lf])
        fsteps.append(p.dma("sync", x_out[:, i * E:(i + 1) * E], xon[:, :], after=[ad]))
    p.step("gpsimd", lambda g: g.engine_nop(), after=[fsteps[-1]])
    return p


def _l0f_host(inputs):
    d = {}
    for l in range(2):
        wqkv = np.asarray(inputs["L0_ffn_wqkv"])[l]
        d[f"wqkv{l}"] = tile_lhsT(wqkv)
        d[f"qs{l}"] = np.asarray(inputs["L0_ffn_q_scale"])[l].reshape(1, 64).astype(np.float32)
        d[f"ks{l}"] = np.asarray(inputs["L0_ffn_k_scale"])[l].reshape(1, 64).astype(np.float32)
        d[f"wo{l}"] = tile_lhsT(np.asarray(inputs["L0_ffn_wo"])[l])
        d[f"lg{l}"] = col128(np.asarray(inputs["L0_ffn_ln_g"])[l])
        d[f"lb{l}"] = col128(np.asarray(inputs["L0_ffn_ln_b"])[l])
        w1 = np.asarray(inputs["L0_ffn_w1"])[l]
        b1 = np.asarray(inputs["L0_ffn_b1"])[l]
        HID = 2730
        a, g = w1[:, 0:HID], w1[:, HID:]
        ba, bg = b1[0:HID], b1[HID:]
        ap = np.zeros((1024, 2816), np.float32); ap[:, :HID] = a
        gp = np.zeros((1024, 2816), np.float32); gp[:, :HID] = g
        bap = np.zeros(2816, np.float32); bap[:HID] = ba
        bgp = np.zeros(2816, np.float32); bgp[:HID] = bg
        w1i = np.empty((1024, 5632), np.float32)
        b1i = np.empty(5632, np.float32)
        for j in range(22):
            w1i[:, (2 * j) * 128:(2 * j + 1) * 128] = ap[:, j * 128:(j + 1) * 128]
            w1i[:, (2 * j + 1) * 128:(2 * j + 2) * 128] = gp[:, j * 128:(j + 1) * 128]
            b1i[(2 * j) * 128:(2 * j + 1) * 128] = bap[j * 128:(j + 1) * 128]
            b1i[(2 * j + 1) * 128:(2 * j + 2) * 128] = bgp[j * 128:(j + 1) * 128]
        d[f"w1{l}"] = tile_lhsT_blockmajor(w1i)
        d[f"b1{l}"] = col128(b1i)
        w2 = np.asarray(inputs["L0_ffn_w2"])[l]
        w2p = np.zeros((2816, 1024), np.float32); w2p[:HID] = w2
        d[f"w2{l}"] = tile_lhsT(w2p)
        d[f"b2{l}"] = np.asarray(inputs["L0_ffn_b2"])[l].reshape(1, 1024).astype(np.float32)
    Dh = 64
    half_idx = np.arange(0, Dh, 2, dtype=np.float64)
    inv = 1.0 / (10000.0 ** (half_idx / Dh))
    t = np.arange(T, dtype=np.float64)
    fr = t[:, None] * inv[None, :]
    freqs = np.concatenate([fr, fr], axis=-1)
    base = (half_idx + 0.4 * Dh) / (1.4 * Dh)
    power = (t - T // 2) / 256.0
    sc = base[None, :] ** power[:, None]
    sc = np.concatenate([sc, sc], axis=-1)
    sgn = np.concatenate([-np.ones(32), np.ones(32)])[None, :]
    QKS = 8.0
    d["ropeAxq"] = nat_tiles((np.cos(freqs) * sc * QKS).astype(np.float32))
    d["ropeBxq"] = nat_tiles((np.sin(freqs) * sc * sgn * QKS).astype(np.float32))
    d["c_idn2"] = np.eye(128, dtype=np.float32)
    d["ropeAxk"] = nat_tiles((np.cos(freqs) / sc).astype(np.float32))
    d["ropeBxk"] = nat_tiles((np.sin(freqs) / sc * sgn).astype(np.float32))
    d["c_idn"] = np.eye(128, dtype=np.float32)
    r = np.arange(128)
    d["c_mlow"] = np.where(r[:, None] >= r[None, :], 0.0, NEG).astype(np.float32)
    d["c_mup"] = np.where(r[:, None] <= r[None, :], 0.0, NEG).astype(np.float32)
    return d


def build_l1_ffn():
    p = Prog("l1f")
    nc = p.nc
    x_in = nc.dram_tensor("x", [128, NT * E], F32, kind="ExternalInput").ap()
    x_out = nc.dram_tensor("xo", [128, NT * E], F32, kind="ExternalOutput").ap()
    wq_d = nc.dram_tensor("wqTi", [128, 16 * 1024], F32, kind="ExternalInput").ap()
    keys_d = nc.dram_tensor("keysTi", [128, 16 * 256], F32, kind="ExternalInput").ap()
    nw_d = nc.dram_tensor("nw", [1, E], F32, kind="ExternalInput").ap()
    tab_d = nc.dram_tensor("tab32", [65536, 2048], F32, kind="ExternalInput").ap()
    tab16 = nc.dram_tensor("tab16", [65536, 2048], BF16, kind="Internal").ap()

    ld = alloc_common(p)
    psA, psB, psC = p.sb["psA"], p.sb["psB"], p.sb["psC"]
    idn, scr, stat = p.sb["idn"], p.sb["scr"], p.sb["stat"]
    W2sb = p.sbuf("W2sb", [128, NE, 4096], BF16)
    xnn = p.sbuf("xnn", [128, NT, E], BF16)
    xnT = p.sbuf("xnT", [128, NE, T], BF16)
    sAB = p.sbuf("sAB", [128, 16, 16], F32)
    iAB = p.sbuf("iAB", [128, 16, 16], I32)
    cwork = p.sbuf("cwork", [128, 112], F32)
    cpack = p.sbuf("cpack", [128, 256], F32)
    w16 = p.sbuf("w16", [128, 16], F32)
    spk = p.sbuf("spk", [128, 256], F32)
    mrep = p.sbuf("mrep", [128, 256], F32)
    dscr = p.sbuf("dscr", [128, 64], I32)
    EIDX = p.sbuf("EIDX", [128, 8, 16], I32)
    RS = p.sbuf("RS", [128, 8, 16], F32)
    hid = p.sbuf("hid", [128, 128], F32)
    cco = p.sbuf("cco", [128, 128], F32)
    biota = p.sbuf("biota", [128, 256], I32)
    biota16f = p.sbuf("biota16f", [128, 16], F32)
    nwt = p.sbuf("nwt", [128, E], F32)
    onesf = p.sbuf("onesf", [1, 128], F32)
    rstat = p.sbuf("rstat", [128, 8], F32)

    ofm = p.step("vector", lambda v: v.memset(onesf[:1, :], 1.0))
    it1 = p.step("gpsimd", lambda g: g.iota(biota[:, :], pattern=[[1, 256]], base=0,
                                            channel_multiplier=0,
                                            allow_small_or_imprecise_dtypes=True))
    it2 = p.step("vector", lambda v: v.tensor_copy(biota16f[:, :], biota[:, 0:16]),
                 after=[it1])
    casts = []
    for ci in range(64):
        casts.append(p.dma("gpsimd", tab16[ci * 1024:(ci + 1) * 1024, :],
                           tab_d[ci * 1024:(ci + 1) * 1024, :]))
    cast_done = casts[-1]

    with ExitStack() as lv1:
        e1 = lv1.enter_context
        wqr = e1(nc.sbuf_tensor("wqr", [128, 4, 1024], BF16))
        kyi = e1(nc.sbuf_tensor("kyi", [128, 16, 256], BF16))
        xdb = e1(nc.sbuf_tensor("xdb", [128, 2, E], F32))
        xlf = e1(nc.sbuf_tensor("xlf", [128, 2, E], F32))
        xnf = e1(nc.sbuf_tensor("xnf", [128, E], F32))
        l_ky = p.dma("gpsimd", kyi[:, :, :], keys_d)
        l_nw = p.dma("sync", xnf[0:1, :], nw_d)
        nwb = p.step("tensor", lambda t: [t.matmul(psC[:, fc * 512:(fc + 1) * 512],
                                                   onesf[:1, :],
                                                   xnf[0:1, fc * 512:(fc + 1) * 512],
                                                   start=True, stop=True)
                                          for fc in range(2)][-1],
                     after=[l_nw, ofm])
        nwe = p.step("vector", lambda v: v.tensor_copy(nwt[:, :], psC[:, 0:E]),
                     after=[nwb])

        apls, lnws, tr_s, trs = [], [], [], []
        for i in range(NT):
            b = i % 2
            o = b * 12
            o2 = b * 4
            xl = p.dma("sync", xdb[:, b, :], x_in[:, i * E:(i + 1) * E],
                       after=[apls[i - 2]] if i >= 2 else [])

            def stats1(v, i=i, o=o, b=b):
                v.scalar_tensor_tensor(scr[:, 0:E], xdb[:, b, :], 1.0, xdb[:, b, :],
                                       ALU.mult, ALU.mult,
                                       accum_out=stat[:, o + 1:o + 2])
                return v.tensor_reduce(stat[:, o:o + 1], xdb[:, b, :], AX.X, ALU.add)
            s1 = p.step("vector", stats1, after=[xl] +
                        ([apls[i - 2]] if i >= 2 else []))

            def stats2(v, i=i, o=o):
                v.tensor_scalar(stat[:, o + 2:o + 3], stat[:, o:o + 1], 1.0 / E,
                                None, ALU.mult)
                return v.tensor_scalar(stat[:, o + 3:o + 4], stat[:, o + 1:o + 2],
                                       1.0 / E, None, ALU.mult)
            s2 = p.step("vector", stats2, after=[s1])

            def stats3(v, i=i, o=o):
                v.scalar_tensor_tensor(stat[:, o + 4:o + 5], stat[:, o + 2:o + 3],
                                       1.0, stat[:, o + 2:o + 3], ALU.mult, ALU.mult)
                return v.tensor_scalar(stat[:, o + 5:o + 6], stat[:, o + 3:o + 4],
                                       1e-5, None, ALU.add)
            s3 = p.step("vector", stats3, after=[s2])
            s4 = p.step("vector",
                        lambda v, o=o: v.tensor_tensor(stat[:, o + 6:o + 7],
                                                       stat[:, o + 5:o + 6],
                                                       stat[:, o + 4:o + 5],
                                                       ALU.subtract), after=[s3])
            a1 = p.step("scalar", lambda s, o=o: s.activation(
                stat[:, o + 7:o + 8], stat[:, o + 6:o + 7], AF.Ln), after=[s4])
            a2 = p.step("scalar", lambda s, o=o: s.activation(
                stat[:, o + 8:o + 9], stat[:, o + 7:o + 8], AF.Exp, scale=-0.5),
                after=[a1])
            a3 = p.step("scalar", lambda s, o=o: s.mul(
                stat[:, o + 9:o + 10], stat[:, o + 2:o + 3], stat[:, o + 8:o + 9]),
                after=[a2])
            a4 = p.step("scalar", lambda s, o=o: s.mul(
                stat[:, o + 10:o + 11], stat[:, o + 9:o + 10], -1.0), after=[a3])

            def apply(s, i=i, o=o, b=b):
                return s.activation(xlf[:, b, :], xdb[:, b, :], AF.Identity,
                                    bias=stat[:, o + 10:o + 11],
                                    scale=stat[:, o + 8:o + 9])
            apls.append(p.step("scalar", apply, after=[a4] +
                               ([apls[i - 2]] if i >= 2 else [])))
            s5 = p.step("vector", lambda v, b=b, o2=o2: v.scalar_tensor_tensor(
                scr[:, 0:E], xlf[:, b, :], 1.0, xlf[:, b, :], ALU.mult, ALU.mult,
                accum_out=rstat[:, o2:o2 + 1]), after=[apls[i], nwe])
            s6 = p.step("vector", lambda v, o2=o2: v.tensor_scalar(
                rstat[:, o2 + 1:o2 + 2], rstat[:, o2:o2 + 1], 1.0 / E, 1e-5,
                ALU.mult, ALU.add), after=[s5])
            a5 = p.step("scalar", lambda s, o2=o2: s.activation(
                rstat[:, o2 + 2:o2 + 3], rstat[:, o2 + 1:o2 + 2], AF.Ln), after=[s6])
            a6 = p.step("scalar", lambda s, o2=o2: s.activation(
                rstat[:, o2 + 3:o2 + 4], rstat[:, o2 + 2:o2 + 3], AF.Exp, scale=-0.5),
                after=[a5])
            a7 = p.step("scalar", lambda s, b=b, o2=o2: s.activation(
                xnf[:, :], xlf[:, b, :], AF.Identity, scale=rstat[:, o2 + 3:o2 + 4]),
                after=[a6, l_nw, nwb] + ([lnws[i - 1]] if i >= 1 else []))
            lnw = p.step("vector", lambda v, i=i: v.tensor_tensor(
                xnn[:, i, :], xnf[:, :], nwt[:, :], ALU.mult), after=[a7])
            lnws.append(lnw)

            def trf(t, i=i, b=b):
                half = b * NE * 128
                for ch in range(NE):
                    ins = t.transpose(psB[:, half + ch * 128: half + (ch + 1) * 128],
                                      xnn[:, i, ch * 128:(ch + 1) * 128], idn[:, :])
                return ins
            tr_s.append(p.step("tensor", trf, after=[lnw, ld["idn"]] +
                               ([trs[i - 2]] if i >= 2 else [])))
            trs.append(p.step("vector", lambda v, i=i, b=b: v.tensor_copy(
                xnT[:, 0:NE, i * 128:(i + 1) * 128],
                psB[:, b * 1024:b * 1024 + 1024]
                .rearrange("p (c q) -> p c q", q=128)), after=[tr_s[i]]))

        w2s = []
        wq_lds = []
        for bb in range(16):
            ldep = [wq_lds[bb - 4], w2s[(bb - 3) * NE - 1]] if bb >= 4 else []
            lw = p.dma("gpsimd", wqr[:, bb % 4, :],
                       wq_d[:, bb * 1024:(bb + 1) * 1024], after=ldep)
            wq_lds.append(lw)
            for eb in range(NE):
                def wm(t, bb=bb, eb=eb):
                    return t.matmul(psC[:, 0:256],
                                    wqr[:, bb % 4, eb * 128:(eb + 1) * 128],
                                    kyi[:, bb, :], start=True, stop=True)
                dep = [lw, l_ky, nwe]
                if w2s:
                    dep.append(w2s[-1])
                m = p.step("tensor", wm, after=dep)
                w2s.append(p.step("vector", lambda v, bb=bb, eb=eb: v.tensor_copy(
                    W2sb[:, eb, bb * 256:(bb + 1) * 256], psC[:, 0:256]),
                    after=[m]))
        w2_done = w2s[-1]

    gbuf = p.sbuf("gbuf", [128, 8, 2048], BF16)
    diagG = p.sbuf("diagG", [128, 2, 8, 128], BF16)
    xrs = p.sbuf("xrs", [128, E], F32)
    xon = p.sbuf("xon", [128, E], F32)
    RNG = 8

    prev_tile = w2_done
    for i in range(NT):
        ab_done = []
        for wv in range(4):
            def smm(t, i=i, wv=wv):
                for ec in range(NE):
                    for hpw in range(4):
                        hp = wv * 4 + hpw
                        ins = t.matmul(psA[:, hpw * 512:hpw * 512 + 256],
                                       xnT[:, ec, i * 128:(i + 1) * 128],
                                       W2sb[:, ec, hp * 256:(hp + 1) * 256],
                                       start=(ec == 0), stop=(ec == NE - 1))
                return ins
            dep = [prev_tile, trs[-1]]
            if ab_done:
                dep.append(ab_done[-1])
            sm = p.step("tensor", smm, after=dep)
            for hpw in range(4):
                hp = wv * 4 + hpw
                s0a = p.step("vector", lambda v, hpw=hpw: v.tensor_copy(
                    spk[:, :], psA[:, hpw * 512:hpw * 512 + 256]), after=[sm, it2])
                s0b0 = p.step("vector", lambda v: v.tensor_scalar(
                    spk[:, :].bitcast(I32), spk[:, :].bitcast(I32), ~0xFF, None,
                    ALU.bitwise_and), after=[s0a])
                s0b = p.step("vector", lambda v: v.tensor_tensor(
                    spk[:, :].bitcast(I32), spk[:, :].bitcast(I32),
                    biota[:, :], ALU.bitwise_or), after=[s0b0])
                s1 = p.step("vector", lambda v: v.max(out=w16[:, 0:8],
                                                      in_=spk[:, :]), after=[s0b])
                s2 = p.step("vector", lambda v: v.match_replace(
                    out=mrep[:, :], in_to_replace=w16[:, 0:8], in_values=spk[:, :],
                    imm_value=-3e38), after=[s1])
                s3 = p.step("vector", lambda v: v.max(out=w16[:, 8:16],
                                                      in_=mrep[:, :]), after=[s2])
                s4 = p.step("vector", lambda v, hp=hp: v.tensor_copy(
                    sAB[:, hp, :], w16[:, :]), after=[s3])
                s5 = p.step("vector", lambda v, hp=hp: v.tensor_scalar(
                    iAB[:, hp, :], sAB[:, hp, :].bitcast(I32), 0xFF, None,
                    ALU.bitwise_and), after=[s4])
                ab_done.append(s5)
        cdone = []
        for h in range(8):
            dep = [ab_done[2 * h + 1] if 2 * h + 1 < len(ab_done) else ab_done[-1]]
            dep = [ab_done[-1]]
            if cdone:
                dep.append(cdone[-1])
            c1 = p.step("vector", lambda v, h=h: v.tensor_tensor(
                cwork[:, 0:64].rearrange("p (a b2) -> p a b2", b2=16),
                sAB[:, 2 * h, 0:4].unsqueeze(2).to_broadcast([128, 4, 16]),
                sAB[:, 2 * h + 1, :].unsqueeze(1).to_broadcast([128, 4, 16]),
                ALU.add), after=dep)
            c2 = p.step("vector", lambda v, h=h: v.tensor_tensor(
                cwork[:, 64:112].rearrange("p (a b2) -> p a b2", b2=4),
                sAB[:, 2 * h, 4:16].unsqueeze(2).to_broadcast([128, 12, 4]),
                sAB[:, 2 * h + 1, 0:4].unsqueeze(1).to_broadcast([128, 12, 4]),
                ALU.add), after=[c1])
            p0 = p.step("vector", lambda v, h=h: v.tensor_scalar(
                dscr[:, 0:16], iAB[:, 2 * h, :], 4, None, ALU.logical_shift_left),
                after=[c2])
            p1 = p.step("vector", lambda v: v.tensor_tensor(
                cpack[:, 0:64].bitcast(I32).rearrange("p (a b2) -> p a b2", b2=16),
                dscr[:, 0:4].unsqueeze(2).to_broadcast([128, 4, 16]),
                biota[:, 0:16].unsqueeze(1).to_broadcast([128, 4, 16]),
                ALU.add), after=[p0])
            p2 = p.step("vector", lambda v: v.tensor_tensor(
                cpack[:, 64:112].bitcast(I32).rearrange("p (a b2) -> p a b2", b2=4),
                dscr[:, 4:16].unsqueeze(2).to_broadcast([128, 12, 4]),
                biota[:, 0:4].unsqueeze(1).to_broadcast([128, 12, 4]),
                ALU.add), after=[p1])
            p3a = p.step("vector", lambda v: v.tensor_scalar(
                spk[:, 0:112].bitcast(I32), cwork[:, 0:112].bitcast(I32),
                ~0xFFF, None, ALU.bitwise_and), after=[p2])
            p3 = p.step("vector", lambda v: v.tensor_tensor(
                cpack[:, 0:112].bitcast(I32), cpack[:, 0:112].bitcast(I32),
                spk[:, 0:112].bitcast(I32), ALU.bitwise_or), after=[p3a])
            m1 = p.step("vector", lambda v: v.max(out=w16[:, 0:8],
                                                  in_=cpack[:, 0:112]), after=[p3])
            m2 = p.step("vector", lambda v: v.match_replace(
                out=mrep[:, 0:112], in_to_replace=w16[:, 0:8],
                in_values=cpack[:, 0:112], imm_value=-3e38), after=[m1])
            m3 = p.step("vector", lambda v: v.max(out=w16[:, 8:16],
                                                  in_=mrep[:, 0:112]), after=[m2])
            d1 = p.step("vector", lambda v: v.tensor_scalar(
                dscr[:, 16:32], w16[:, :].bitcast(I32), 0xFFF, None,
                ALU.bitwise_and), after=[m3])
            d2 = p.step("vector", lambda v: v.tensor_scalar(
                dscr[:, 32:48], dscr[:, 16:32], 4, None, ALU.arith_shift_right),
                after=[d1])
            d3 = p.step("vector", lambda v: v.tensor_scalar(
                dscr[:, 48:64], dscr[:, 16:32], 0xF, None, ALU.bitwise_and),
                after=[d2])
            sc1 = p.step("vector", lambda v, h=h: v.tensor_scalar(
                RS[:, h, :].bitcast(I32), w16[:, :].bitcast(I32), ~0xFFF, None,
                ALU.bitwise_and), after=[d3])
            sc2 = p.step("vector", lambda v, h=h: v.tensor_scalar(
                RS[:, h, :], RS[:, h, :], 0.0, None, ALU.max), after=[sc1])
            r1 = p.step("vector", lambda v: v.tensor_copy(
                cwork[:, 0:16], dscr[:, 48:64]), after=[sc2])
            r2 = p.step("vector", lambda v: v.tensor_tensor(
                cpack[:, 0:256].rearrange("p (r b2) -> p r b2", b2=16),
                cwork[:, 0:16].unsqueeze(2).to_broadcast([128, 16, 16]),
                biota16f[:, 0:16].unsqueeze(1).to_broadcast([128, 16, 16]),
                ALU.is_equal), after=[r1])
            r3 = p.step("vector", lambda v, h=h: v.tensor_copy(
                cwork[:, 16:32], iAB[:, 2 * h + 1, :]), after=[r2])
            r4 = p.step("vector", lambda v: v.tensor_tensor(
                cpack[:, 0:256].rearrange("p (r b2) -> p r b2", b2=16),
                cpack[:, 0:256].rearrange("p (r b2) -> p r b2", b2=16),
                cwork[:, 16:32].unsqueeze(1).to_broadcast([128, 16, 16]),
                ALU.mult), after=[r3])
            r5 = p.step("vector", lambda v: v.tensor_reduce(
                cwork[:, 32:48],
                cpack[:, 0:256].rearrange("p (r b2) -> p r b2", b2=16),
                AX.X, ALU.add), after=[r4])
            e1_ = p.step("vector", lambda v: v.tensor_scalar(
                dscr[:, 32:48], dscr[:, 32:48], 8, None, ALU.logical_shift_left),
                after=[r5])
            e2_ = p.step("vector", lambda v: v.tensor_copy(
                dscr[:, 0:16], cwork[:, 32:48]), after=[e1_])
            e3_ = p.step("vector", lambda v, h=h: v.tensor_tensor(
                EIDX[:, h, :], dscr[:, 32:48], dscr[:, 0:16], ALU.add), after=[e2_])
            cdone.append(e3_)

        gsteps, dsteps, usteps = {}, {}, {}
        up_prev = None
        for j in range(128):
            hh, kk = j // 16, j % 16
            dep = [cdone[-1], cast_done]
            if j >= RNG:
                dep.append(usteps[j - RNG])
            gsteps[j] = p.step("gpsimd", lambda g, j=j, hh=hh, kk=kk:
                               g.indirect_dma_start(
                                   out=gbuf[:, j % RNG, :], out_offset=None,
                                   in_=tab16[:, :],
                                   in_offset=IndirectOffsetOnAxis(
                                       EIDX[:, hh, kk:kk + 1], 0)),
                               after=dep, dma=True)
            dsteps[j] = p.step("vector", lambda v, i=i, j=j: v.scalar_tensor_tensor(
                scr[:, 0:E], gbuf[:, j % RNG, 0:1024], 1.0, xnn[:, i, :],
                ALU.mult, ALU.mult, accum_out=hid[:, j:j + 1]), after=[gsteps[j]])
            if j % 8 == 7:
                g8 = j // 8
                ge1 = p.step("scalar", lambda s, g8=g8: s.activation(
                    cco[:, g8 * 8:g8 * 8 + 8], hid[:, g8 * 8:g8 * 8 + 8], AF.Gelu),
                    after=[dsteps[j]])
                ge2 = p.step("vector", lambda v, g8=g8: v.tensor_tensor(
                    cco[:, g8 * 8:g8 * 8 + 8], cco[:, g8 * 8:g8 * 8 + 8],
                    RS[:, :, :].rearrange("p h k -> p (h k)")[:, g8 * 8:g8 * 8 + 8],
                    ALU.mult), after=[ge1])
                dep = [ge2, ld["idn"]]
                if g8 >= 2:
                    dep.append(usteps[(g8 - 2) * 8])
                ge3 = p.step("vector", lambda v, g8=g8: v.tensor_tensor(
                    diagG[:, g8 % 2, :, :],
                    cco[:, g8 * 8:g8 * 8 + 8].unsqueeze(2).to_broadcast([128, 8, 128]),
                    idn[:, :].unsqueeze(1).to_broadcast([128, 8, 128]),
                    ALU.mult), after=dep)

                def upf(t, i=i, g8=g8):
                    for jj in range(8):
                        j2 = g8 * 8 + jj
                        for fc in range(2):
                            ins = t.matmul(psC[:, fc * 512:(fc + 1) * 512],
                                           diagG[:, g8 % 2, jj, :],
                                           gbuf[:, j2 % RNG, 1024 + fc * 512:
                                                1024 + (fc + 1) * 512],
                                           start=(j2 == 0), stop=(j2 == 127),
                                           skip_group_check=True)
                    return ins
                dep = [ge3]
                if up_prev is not None:
                    dep.append(up_prev)
                us = p.step("tensor", upf, after=dep)
                for jj in range(8):
                    usteps[g8 * 8 + jj] = us
                up_prev = us

        xr = p.dma("sync", xrs[:, :], x_in[:, i * E:(i + 1) * E], after=[prev_tile])
        ad = p.step("vector", lambda v: v.tensor_tensor(xon[:, :], psC[:, 0:E],
                                                        xrs[:, :], ALU.add),
                    after=[up_prev, xr])
        st = p.dma("sync", x_out[:, i * E:(i + 1) * E], xon[:, :], after=[ad])
        prev_tile = st
    p.step("gpsimd", lambda g: g.engine_nop(), after=[prev_tile])
    return p


def _l1f_host(inputs):
    d = {}
    wq = np.asarray(inputs["L1_ffn_wq"], dtype=np.float32)
    d["wqTi"] = tile_lhsT(np.ascontiguousarray(wq.T))
    keys = np.asarray(inputs["L1_ffn_keys"], dtype=np.float32)
    kimg = np.zeros((128, 16, 256), np.float32)
    for b in range(16):
        kimg[:, b, :] = keys[b // 2, :, b % 2, :].T
    d["keysTi"] = kimg.reshape(128, 16 * 256)
    d["nw"] = np.asarray(inputs["L1_ffn_norm_w"], dtype=np.float32).reshape(1, E)
    dn = np.asarray(inputs["L1_ffn_expert_down"], dtype=np.float32)
    up = np.asarray(inputs["L1_ffn_expert_up"], dtype=np.float32)
    d["tab32"] = np.ascontiguousarray(np.concatenate([dn, up], axis=1))
    d["c_idn"] = np.eye(128, dtype=np.float32)
    return d
